# version 20
# baseline (speedup 1.0000x reference)
import os
import numpy as np
import ml_dtypes
import concourse.bass as bass
import concourse.mybir as mybir
from contextlib import ExitStack

SERIALIZE = int(os.environ.get("KSER", "0"))

F32 = mybir.dt.float32
BF16 = mybir.dt.bfloat16
AF = mybir.ActivationFunctionType
ALU = mybir.AluOpType
AX = mybir.AxisListType

B, T, C = 2, 2048, 1024
NTOK = B * T
NCH = C // 128
NTT = NTOK // 128
NH, HD, DROPE, LAT = 16, 64, 16, 256
DH = 4096
ED, NE = 512, 8

ENGS = ("g", "t", "v", "s", "y")


NCHAIN = 4


class Op:
    __slots__ = ("eng", "fn", "idx", "inc_idx", "dma", "needed", "deps", "lane")

    def __init__(self, eng, fn, dma, deps):
        self.eng = eng
        self.fn = fn
        self.dma = dma
        self.deps = deps
        self.idx = -1
        self.inc_idx = -1
        self.needed = dma
        self.lane = None


class Sched:

    def __init__(self, ctx, name):
        self.ctx = ctx
        self.name = name
        self.ops = {e: [] for e in ENGS}
        self.all_ops = []
        self.last_writer = {}
        self.readers = {}
        self.phase_deps = []
        self.dma_count = {}
        self.chain_prev = {}

    def fence_all(self):
        self.phase_deps = [self.ops[e][-1] for e in ENGS if self.ops[e]]

    def add(self, eng, fn, reads=(), writes=(), dma=False, extra_deps=()):
        deps = set()
        if dma:
            nq = self.dma_count.get(eng, 0)
            self.dma_count[eng] = nq + 1
            chain = nq % NCHAIN
            prev = self.chain_prev.get((eng, chain))
            if prev is not None:
                deps.add(prev)
        for r in reads:
            w = self.last_writer.get(r)
            if w is not None:
                deps.add(w)
        for r in writes:
            w = self.last_writer.get(r)
            if w is not None:
                deps.add(w)
            for rd in self.readers.get(r, ()):
                deps.add(rd)
        for d in extra_deps:
            if d is not None:
                deps.add(d)
        for d in self.phase_deps:
            deps.add(d)
        if SERIALIZE == 1 and self.all_ops:
            deps.add(self.all_ops[-1])
        elif SERIALIZE == 2 and self.all_ops:
            if eng == "t" or self.all_ops[-1].eng == "t":
                deps.update(o2 for o2 in self.all_ops[-40:])
        elif SERIALIZE == 3 and self.all_ops:
            if dma or self.all_ops[-1].dma:
                deps.update(o2 for o2 in self.all_ops[-40:])
        elif SERIALIZE == 4 and dma and self.ops[eng]:
            prev = [o2 for o2 in self.ops[eng] if o2.dma]
            if prev:
                deps.add(prev[-1])
        op = Op(eng, fn, dma, deps)
        op.lane = (eng, chain) if dma else (eng,)
        if dma:
            self.chain_prev[(eng, chain)] = op
        op.idx = len(self.ops[eng])
        self.ops[eng].append(op)
        self.all_ops.append(op)
        for r in writes:
            self.last_writer[r] = op
            self.readers[r] = []
        for r in reads:
            self.readers.setdefault(r, []).append(op)
        return op

    def load(self, dst_ap, src_ap, reads=(), writes=(), q="g"):
        return self.add(q, lambda e: e.dma_start(out=dst_ap, in_=src_ap),
                        reads=reads, writes=writes, dma=True)

    def store(self, dst_ap, src_ap, reads=(), writes=(), q="y"):
        return self.add(q, lambda e: e.dma_start(out=dst_ap, in_=src_ap),
                        reads=reads, writes=writes, dma=True)

    def flush(self):
        ctx = self.ctx
        for op in self.all_ops:
            for d in op.deps:
                d.needed = True
        for e in ENGS:
            if self.ops[e]:
                self.ops[e][-1].needed = True
        lane_n = ctx.lane_n
        lanes_used = set()
        for e in ENGS:
            for op in self.ops[e]:
                if op.needed:
                    lane_n[op.lane] = lane_n.get(op.lane, 0) + (16 if op.dma else 1)
                    lanes_used.add(op.lane)
                op.inc_idx = lane_n.get(op.lane, 0)
        totals = {ln: lane_n[ln] for ln in lanes_used}
        sems = {}
        for ln in lanes_used:
            sems[ln] = ctx.lane_sem(ln)
        with ctx.nc.Block() as block:
            eng_of = {"g": block.gpsimd, "t": block.tensor, "v": block.vector,
                      "s": block.scalar, "y": block.sync}
            for e in ENGS:
                ops = self.ops[e]
                if not ops:
                    continue

                def emit(eng, ops=ops, e=e):
                    if ctx.fval > 0:
                        eng.wait_ge(ctx.fence, ctx.fval)
                    waited = {}
                    for op in ops:
                        need = {}
                        for d in op.deps:
                            need[d.lane] = max(need.get(d.lane, 0), d.inc_idx)
                        for key, val in need.items():
                            if waited.get(key, 0) < val:
                                eng.wait_ge(sems[key], val)
                                waited[key] = val
                        inst = op.fn(eng)
                        if op.needed:
                            inst.then_inc(sems[op.lane], 16 if op.dma else 1)

                eng_of[e](emit)

        def closer(eng):
            for ln, tot in totals.items():
                eng.wait_ge(sems[ln], tot)
            eng.nop().then_inc(ctx.fence, 1)

        with ctx.nc.Block() as block2:
            block2.gpsimd(closer)
        ctx.fval += 1


class Ctx:
    def __init__(self, nc):
        self.nc = nc
        self.es = ExitStack()
        self.fence = self.es.enter_context(nc.semaphore("fence"))
        self.fval = 0
        self.uid = 0
        self.lane_n = {}
        self.lane_sems = {}

    def sem(self, name):
        self.uid += 1
        return self.es.enter_context(self.nc.semaphore(f"{name}_{self.uid}"))

    def lane_sem(self, lane):
        if lane not in self.lane_sems:
            self.lane_sems[lane] = self.sem("L" + "".join(str(x) for x in lane))
        return self.lane_sems[lane]

    def name(self, n):
        self.uid += 1
        return f"{n}{self.uid}_"


def _fix_flush(sched):
    sched.flush()



def make_identity_bf(nc, sc, ident):
    def f1(e):
        e.memset(ident[:], 0.0)
        return e.affine_select(out=ident[:], in_=ident[:], compare_op=ALU.not_equal,
                               fill=1.0, base=0, pattern=[[-1, 128]], channel_multiplier=1)
    return sc.add("g", f1, writes=("ident",))


def build_ln_section(ctx, d, src_name, xn_f32_name, xnT_name, hT_name, out_pref):
    nc = ctx.nc
    nm = ctx.name(out_pref)
    sc = Sched(ctx, nm)
    src = d[src_name]
    with ExitStack() as ses:
        sb = lambda n, s, dt: ses.enter_context(nc.sbuf_tensor(nm + n, s, dt))
        pb = lambda n, s, dt=F32: ses.enter_context(nc.psum_tensor(nm + n, s, dt))
        xa = [sb(f"xa{j}", [128, C], F32) for j in range(2)]
        sq = [sb(f"sq{j}", [128, C], F32) for j in range(2)]
        xn_t = [sb(f"xn{j}", [128, C], F32) for j in range(2)]
        xnb = [sb(f"xnb{j}", [128, C], BF16) for j in range(2)]
        hb = [sb(f"hb{j}", [128, C], BF16) for j in range(2)]
        mus = sb("mus", [128, NTT], F32)
        sqs = sb("sqs", [128, NTT], F32)
        rstd = sb("rstd", [128, NTT], F32)
        msum = sb("msum", [128, NTT], F32)
        rstd2 = sb("rstd2", [128, NTT], F32)
        ident = sb("ident", [128, 128], BF16)
        tp = [pb(f"tp{j}", [128, 1024], BF16) for j in range(2)]
        slab = [sb(f"sl{j}", [128, NCH, 512], BF16) for j in range(2)]
        slab2 = [sb(f"sl2{j}", [128, NCH, 512], BF16) for j in range(2)]

        make_identity_bf(nc, sc, ident)

        for i in range(NTT):
            j = i % 2
            ld = sc.load(xa[j][:], src[i * 128:(i + 1) * 128, :],
                         writes=(f"xa{j}",))
            sc.add("v", lambda e, j=j, i=i: e.reduce_sum(mus[:, i:i + 1], xa[j][:], axis=AX.X),
                   reads=(f"xa{j}",), writes=(f"mu{i}",))
            sc.add("v", lambda e, j=j: e.tensor_mul(sq[j][:], xa[j][:], xa[j][:]),
                   reads=(f"xa{j}",), writes=(f"sq{j}",))
            sc.add("v", lambda e, j=j, i=i: e.reduce_sum(sqs[:, i:i + 1], sq[j][:], axis=AX.X),
                   reads=(f"sq{j}", f"xa{j}"), writes=(f"sqs{i}",))
        sc.add("v", lambda e: e.tensor_scalar(mus[:], mus[:], 1.0 / C, None, ALU.mult),
               reads=tuple(f"mu{i}" for i in range(NTT)), writes=("mus",))
        sc.add("v", lambda e: e.tensor_scalar(sqs[:], sqs[:], 1.0 / C, None, ALU.mult),
               reads=tuple(f"sqs{i}" for i in range(NTT)), writes=("sqs",))
        sc.add("v", lambda e: e.tensor_mul(rstd[:], mus[:], mus[:]),
               reads=("mus",), writes=("rstd",))
        sc.add("v", lambda e: e.tensor_tensor(rstd[:], sqs[:], rstd[:], ALU.subtract),
               reads=("sqs", "rstd"), writes=("rstd",))
        sc.add("v", lambda e: e.tensor_scalar(rstd[:], rstd[:], 1e-5, None, ALU.add),
               reads=("rstd",), writes=("rstd",))
        sc.add("s", lambda e: e.activation(rstd[:], rstd[:], AF.Sqrt),
               reads=("rstd",), writes=("rstd",))
        sc.add("v", lambda e: e.reciprocal(rstd[:], rstd[:]),
               reads=("rstd",), writes=("rstd",))

        want_h = hT_name is not None
        for i in range(NTT):
            j = i % 2
            sc.load(xa[j][:], src[i * 128:(i + 1) * 128, :], writes=(f"xa{j}",))
            sc.add("v", lambda e, j=j, i=i: e.tensor_scalar(
                xn_t[j][:], xa[j][:], mus[:, i:i + 1], rstd[:, i:i + 1],
                ALU.subtract, ALU.mult),
                reads=(f"xa{j}", "mus", "rstd"), writes=(f"xn{j}",))
            if xn_f32_name is not None:
                sc.store(d[xn_f32_name][i * 128:(i + 1) * 128, :], xn_t[j][:],
                         reads=(f"xn{j}",), writes=(f"dxn{i}",))
            sc.add("v", lambda e, j=j: e.tensor_copy(xnb[j][:], xn_t[j][:]),
                   reads=(f"xn{j}",), writes=(f"xnb{j}",))
            if want_h:
                sc.add("v", lambda e, j=j: e.tensor_mul(sq[j][:], xn_t[j][:], xn_t[j][:]),
                       reads=(f"xn{j}",), writes=(f"sq{j}",))
                sc.add("v", lambda e, j=j, i=i: e.reduce_sum(msum[:, i:i + 1], sq[j][:], axis=AX.X),
                       reads=(f"sq{j}",), writes=(f"ms{i}",))
            si = (i // 4) % 2
            for c in range(NCH):
                sc.add("t", lambda e, j=j, c=c: e.transpose(
                    tp[j][:, c * 128:(c + 1) * 128], xnb[j][:, c * 128:(c + 1) * 128], ident[:]),
                    reads=(f"xnb{j}", "ident"), writes=(f"tp{j}_{c}",))
            sc.add("v", lambda e, j=j, si=si, i=i: e.tensor_copy(
                slab[si][:, :, (i % 4) * 128:(i % 4 + 1) * 128],
                tp[j][:].rearrange("p (c f) -> p c f", c=NCH)),
                reads=tuple(f"tp{j}_{c}" for c in range(NCH)),
                writes=(f"slabw{si}_{i % 4}",), extra_deps=())
            if i % 4 == 3:
                i0 = i - 3
                sc.store(d[xnT_name][:, :, i0 * 128:(i0 + 4) * 128], slab[si][:],
                         reads=tuple(f"slabw{si}_{k}" for k in range(4)))
        if want_h:
            sc.add("v", lambda e: e.tensor_scalar(msum[:], msum[:], 1.0 / C, 1e-6, ALU.mult, ALU.add),
                   reads=tuple(f"ms{i}" for i in range(NTT)), writes=("msum",))
            sc.add("s", lambda e: e.activation(rstd2[:], msum[:], AF.Sqrt),
                   reads=("msum",), writes=("rstd2",))
            sc.add("v", lambda e: e.reciprocal(rstd2[:], rstd2[:]),
                   reads=("rstd2",), writes=("rstd2",))
            for i in range(NTT):
                j = i % 2
                ld = sc.load(xa[j][:], d[xn_f32_name][i * 128:(i + 1) * 128, :],
                             reads=(f"dxn{i}",), writes=(f"xa{j}",))
                sc.add("v", lambda e, j=j, i=i: e.tensor_scalar(
                    hb[j][:], xa[j][:], rstd2[:, i:i + 1], None, ALU.mult),
                    reads=(f"xa{j}", "rstd2"), writes=(f"hb{j}",))
                si = (i // 4) % 2
                for c in range(NCH):
                    sc.add("t", lambda e, j=j, c=c: e.transpose(
                        tp[j][:, c * 128:(c + 1) * 128], hb[j][:, c * 128:(c + 1) * 128], ident[:]),
                        reads=(f"hb{j}", "ident"), writes=(f"tp{j}_{c}",))
                sc.add("v", lambda e, j=j, si=si, i=i: e.tensor_copy(
                    slab2[si][:, :, (i % 4) * 128:(i % 4 + 1) * 128],
                    tp[j][:].rearrange("p (c f) -> p c f", c=NCH)),
                    reads=tuple(f"tp{j}_{c}" for c in range(NCH)),
                    writes=(f"slab2w{si}_{i % 4}",))
                if i % 4 == 3:
                    i0 = i - 3
                    sc.store(d[hT_name][:, :, i0 * 128:(i0 + 4) * 128], slab2[si][:],
                             reads=tuple(f"slab2w{si}_{k}" for k in range(4)))
        sc.flush()
    return


FEAT = "feat"
TOK = "tok"

DRAM_SPECS = {
    "xn_f32":  ([NTOK, C], F32),
    "xnT_bf":  ([128, NCH, NTOK], BF16),
    "hT_bf":   ([128, NCH, NTOK], BF16),
    "kT_f32":  ([128, NCH, NTOK], F32),
    "vT_f32":  ([128, NCH, NTOK], F32),
    "rT_f32":  ([128, NCH, NTOK], F32),
    "x1_f32":  ([NTOK, C], F32),
    "f0_f32":  ([NTOK, C], F32),
    "h2T_bf":  ([128, NCH, NTOK], BF16),
    "k2T_bf":  ([128, DH // 128, NTOK], BF16),
    "r2_f32":  ([NTOK, C], F32),
    "xb0a_f32": ([NTOK, C], F32),
    "out0_f32": ([NTOK, C], F32),
    "xn1T_bf": ([128, NCH, NTOK], BF16),
    "qT_bf":   ([128, NCH, NTOK], BF16),
    "kTq_bf":  ([128, NCH, NTOK], BF16),
    "v_bf":    ([NTOK, C], BF16),
    "P_bf":    ([T, T], BF16),
    "P2_bf":   ([T, T], BF16),
    "P3_bf":   ([T, T], BF16),
    "P4_bf":   ([T, T], BF16),
    "OT_bf":   ([128, NCH, NTOK], BF16),
    "att_f32": ([NTOK, C], F32),
    "hfb_bf":  ([128, DH // 128, NTOK], BF16),
    "g1_bf":   ([128, DH // 128, NTOK], BF16),
    "g2_bf":   ([128, DH // 128, NTOK], BF16),
}

WEIGHT_SPECS = {
    "W_key": ([C, C], F32), "W_value": ([C, C], F32), "W_recept": ([C, C], F32),
    "W_out": ([C, C], F32), "W_key2": ([C, DH], F32), "W_value2": ([DH, C], F32),
    "W_recept2": ([C, C], F32),
    "Ws1": ([C, ED], F32), "Ws2": ([ED, C], F32),
    "W1": ([NE, C, ED], F32), "W2": ([NE, ED, C], F32), "Wr": ([C, NE], F32),
    "wqT": ([C, C], F32), "wk_aT": ([C, LAT], F32), "wk_bT": ([LAT, C], F32),
    "wv_aT": ([C, LAT], F32), "wv_bT": ([LAT, C], F32), "woT": ([C, C], F32),
    "w1T": ([C, DH], F32), "w2T": ([C, DH], F32),
    "h1T": ([DH, DH], F32), "h2T": ([DH, DH], F32), "w3T": ([DH, C], F32),
    "w1s": ([DH], F32), "w2s": ([DH], F32), "h1s": ([DH], F32), "h2s": ([DH], F32),
    "w3s": ([C], F32),
    "ln1_w": ([C], F32), "ln2_w": ([C], F32),
    "time_decay": ([C], F32), "time_first": ([C], F32),
    "tmk": ([C], F32), "tmv": ([C], F32), "tmr": ([C], F32),
    "tmk2": ([C], F32), "tmr2": ([C], F32),
    "crep": ([T, 128], F32), "srep": ([T, 128], F32),
}


def declare_drams(nc, inspect=()):
    d = {}
    d["x"] = nc.dram_tensor("x", [NTOK, C], F32, kind="ExternalInput").ap()
    d["out"] = nc.dram_tensor("out", [NTOK, C], F32, kind="ExternalOutput").ap()
    for n, (sh, dt) in WEIGHT_SPECS.items():
        d[n] = nc.dram_tensor(n, sh, dt, kind="ExternalInput").ap()
    for n, (sh, dt) in DRAM_SPECS.items():
        kind = "ExternalOutput" if n in inspect else "Internal"
        d[n] = nc.dram_tensor(n, sh, dt, kind=kind).ap()
    return d


def rope_perm():
    p = []
    for h in range(NH):
        base = h * HD
        p += [base + 2 * j for j in range(8)]
        p += [base + 2 * j + 1 for j in range(8)]
        p += [base + j for j in range(DROPE, HD)]
    return np.array(p)


def host_prep(inp):
    P = rope_perm()
    asc = np.ascontiguousarray
    w = {
        "W_key": inp["L0_attn_W_key"], "W_value": inp["L0_attn_W_value"],
        "W_recept": inp["L0_attn_W_recept"], "W_out": inp["L0_attn_W_out"],
        "W_key2": inp["L0_attn_W_key2"], "W_value2": inp["L0_attn_W_value2"],
        "W_recept2": inp["L0_attn_W_recept2"],
        "Ws1": inp["L0_ffn_Ws1"], "Ws2": inp["L0_ffn_Ws2"],
        "W1": inp["L0_ffn_W1"], "W2": inp["L0_ffn_W2"], "Wr": inp["L0_ffn_Wr"],
        "wqT": asc(inp["L1_attn_wq"][P, :].T),
        "wk_aT": asc(inp["L1_attn_wk_a"].T),
        "wk_bT": asc(inp["L1_attn_wk_b"][P, :].T),
        "wv_aT": asc(inp["L1_attn_wv_a"].T),
        "wv_bT": asc(inp["L1_attn_wv_b"].T),
        "woT": asc(inp["L1_attn_wo"].T),
        "w1T": asc(inp["L1_ffn_w1_w"].T), "w2T": asc(inp["L1_ffn_w2_w"].T),
        "h1T": asc(inp["L1_ffn_h1_w"].T), "h2T": asc(inp["L1_ffn_h2_w"].T),
        "w3T": asc(inp["L1_ffn_w3_w"].T),
        "w1s": inp["L1_ffn_w1_s"], "w2s": inp["L1_ffn_w2_s"],
        "h1s": inp["L1_ffn_h1_s"], "h2s": inp["L1_ffn_h2_s"],
        "w3s": inp["L1_ffn_w3_s"],
        "ln1_w": inp["L0_attn_ln1_w"], "ln2_w": inp["L0_attn_ln2_w"],
        "time_decay": inp["L0_attn_time_decay"], "time_first": inp["L0_attn_time_first"],
        "tmk": inp["L0_attn_time_mix_k"], "tmv": inp["L0_attn_time_mix_v"],
        "tmr": inp["L0_attn_time_mix_r"],
        "tmk2": inp["L0_attn_time_mix_k2"], "tmr2": inp["L0_attn_time_mix_r2"],
    }
    inv = (1.0 / 10000.0 ** (np.arange(0, HD, 2, dtype=np.float32) / HD))[:8]
    ang = np.arange(NH, dtype=np.float32)[:, None] * inv[None, :]
    crow = np.cos(ang).reshape(-1).astype(np.float32)
    srow = np.sin(ang).reshape(-1).astype(np.float32)
    w["crep"] = asc(np.tile(crow[None, :], (T, 1)))
    w["srep"] = asc(np.tile(srow[None, :], (T, 1)))
    w = {k: np.ascontiguousarray(np.asarray(v, dtype=np.float32)) for k, v in w.items()}
    x = np.asarray(inp["L0_attn_x"], dtype=np.float32)
    shards = [np.ascontiguousarray(x[2 * c:2 * c + 2].reshape(NTOK, C)) for c in range(8)]
    return w, shards


def col_loads(sc, dst, vec_ap, rname):
    for c in range(dst.shape[1]):
        sc.load(dst[:, c:c + 1], vec_ap[c * 128:(c + 1) * 128].unsqueeze(1),
                writes=(f"{rname}{c}",))


def build_kvr(ctx, d):
    nc = ctx.nc
    nm = ctx.name("kvr")
    sc = Sched(ctx, nm)
    with ExitStack() as ses:
        sb = lambda n, s, dt: ses.enter_context(nc.sbuf_tensor(nm + n, s, dt))
        pb = lambda n, s, dt=F32: ses.enter_context(nc.psum_tensor(nm + n, s, dt))
        ln1 = sb("ln1", [128, NCH], F32)
        tm = {p: sb(p, [128, NCH], F32) for p in ("tmk", "tmv", "tmr")}
        om = {p: sb("o" + p, [128, NCH], F32) for p in ("tmk", "tmv", "tmr")}
        W = {p: sb("W" + p, [128, NCH, C], BF16) for p in ("k", "v", "r")}
        hT = sb("hT", [128, T], BF16)
        tmp = sb("tmp", [128, T], F32)
        mix = {p: sb("mx" + p, [128, NCH, T], BF16) for p in ("k", "v", "r")}
        ps = [pb(f"ps{j}", [128, 512]) for j in range(2)]
        ev = [sb(f"ev{j}", [128, 512], F32) for j in range(2)]

        col_loads(sc, ln1, d["ln1_w"], "ln1c")
        for p, src in (("tmk", "tmk"), ("tmv", "tmv"), ("tmr", "tmr")):
            col_loads(sc, tm[p], d[p], p + "c")
            sc.add("v", lambda e, p=p: e.tensor_scalar(om[p][:], tm[p][:], -1.0, 1.0, ALU.mult, ALU.add),
                   reads=tuple(f"{p}c{c}" for c in range(NCH)), writes=("o" + p,))
        for p, wn in (("k", "W_key"), ("v", "W_value"), ("r", "W_recept")):
            for ci in range(NCH):
                sc.load(W[p][:, ci, :], d[wn][ci * 128:(ci + 1) * 128, :],
                        writes=(f"W{p}{ci}",))
                sc.add("v", lambda e, p=p, ci=ci: e.tensor_scalar(
                    W[p][:, ci, :], W[p][:, ci, :], ln1[:, ci:ci + 1], None, ALU.mult),
                    reads=(f"W{p}{ci}",) + tuple(f"ln1c{c}" for c in range(NCH)),
                    writes=(f"W{p}{ci}",))

        for b in range(B):
            sc.fence_all()
            for c in range(NCH):
                sc.load(hT[:], d["hT_bf"][:, c, b * T:(b + 1) * T], writes=("hT",))
                for p, pr in (("k", "tmk"), ("v", "tmv"), ("r", "tmr")):
                    sc.add("v", lambda e, p=p, pr=pr, c=c: e.tensor_scalar(
                        tmp[:], hT[:], tm[pr][:, c:c + 1], None, ALU.mult),
                        reads=("hT",), writes=("tmp",))
                    sc.add("v", lambda e, p=p, pr=pr, c=c: e.scalar_tensor_tensor(
                        mix[p][:, c, 1:T], hT[:, 0:T - 1], om[pr][:, c:c + 1],
                        tmp[:, 1:T], ALU.mult, ALU.add),
                        reads=("hT", "tmp", "o" + pr), writes=(f"mix{p}{c}",))
                    sc.add("v", lambda e, p=p, c=c: e.tensor_copy(
                        mix[p][:, c, 0:1], tmp[:, 0:1]),
                        reads=("tmp",), writes=(f"mix{p}{c}",))
            jj = 0
            for p, outn, sig in (("k", "kT_f32", False), ("v", "vT_f32", False),
                                 ("r", "rT_f32", True)):
                for co in range(NCH):
                    for n in range(T // 512):
                        j = jj % 2
                        jj += 1

                        def mmf(e, p=p, co=co, n=n, j=j):
                            for ci in range(NCH):
                                mm = e.matmul(ps[j][:], W[p][:, ci, co * 128:(co + 1) * 128],
                                              mix[p][:, ci, n * 512:(n + 1) * 512],
                                              start=(ci == 0), stop=(ci == NCH - 1))
                            return mm
                        sc.add("t", mmf,
                               reads=tuple(f"W{p}{ci}" for ci in range(NCH)) +
                                     tuple(f"mix{p}{ci}" for ci in range(NCH)),
                               writes=(f"ps{j}",))
                        if sig:
                            sc.add("s", lambda e, j=j: e.activation(ev[j][:], ps[j][:], AF.Sigmoid),
                                   reads=(f"ps{j}",), writes=(f"ev{j}",))
                        else:
                            sc.add("v", lambda e, j=j: e.tensor_copy(ev[j][:], ps[j][:]),
                                   reads=(f"ps{j}",), writes=(f"ev{j}",))
                        sc.store(d[outn][:, co, b * T + n * 512: b * T + (n + 1) * 512],
                                 ev[j][:], reads=(f"ev{j}",))
        sc.flush()


def build_wkv(ctx, d):
    nc = ctx.nc
    nm = ctx.name("wkv")
    sc = Sched(ctx, nm)
    with ExitStack() as ses:
        sb = lambda n, s, dt: ses.enter_context(nc.sbuf_tensor(nm + n, s, dt))
        pb = lambda n, s, dt=F32: ses.enter_context(nc.psum_tensor(nm + n, s, dt))
        rw = sb("rw", [128, NCH, NTOK], BF16)
        Wo = sb("Wo", [128, NCH, C], BF16)
        ew = sb("ew", [128, NCH], F32)
        uu = sb("uu", [128, NCH], F32)
        for ci in range(NCH):
            sc.load(Wo[:, ci, :], d["W_out"][ci * 128:(ci + 1) * 128, :],
                    writes=(f"Wo{ci}",))
        col_loads(sc, ew, d["time_decay"], "td")
        col_loads(sc, uu, d["time_first"], "uc")
        sc.add("s", lambda e: e.activation(ew[:], ew[:], AF.Exp),
               reads=tuple(f"td{c}" for c in range(NCH)), writes=("ew",))
        sc.add("v", lambda e: e.tensor_scalar(ew[:], ew[:], -1.0, None, ALU.mult),
               reads=("ew",), writes=("ew",))
        sc.add("v", lambda e: e.tensor_scalar(ew[:], ew[:], 20.0, -20.0, ALU.min, ALU.max),
               reads=("ew",), writes=("ew",))
        sc.add("s", lambda e: e.activation(ew[:], ew[:], AF.Exp),
               reads=("ew",), writes=("ew",))

        with ExitStack() as scan_scope:
            sb2 = lambda n, s, dt: scan_scope.enter_context(nc.sbuf_tensor(nm + n, s, dt))
            kt = [sb2(f"kt{j}", [128, T], F32) for j in range(2)]
            vt = [sb2(f"vt{j}", [128, T], F32) for j in range(2)]
            rt = [sb2(f"rt{j}", [128, T], F32) for j in range(2)]
            e1 = sb2("e1", [128, T], F32)
            ekt = sb2("ekt", [128, T], F32)
            ekv = sb2("ekv", [128, T], F32)
            ektv = sb2("ektv", [128, T], F32)
            At = sb2("At", [128, T], F32)
            Bt = sb2("Bt", [128, T], F32)
            den = sb2("den", [128, T], F32)

            for it in range(NCH * B):
                c, b = it // B, it % B
                j = it % 2
                sc.load(kt[j][:], d["kT_f32"][:, c, b * T:(b + 1) * T], writes=(f"kt{j}",))
                sc.load(vt[j][:], d["vT_f32"][:, c, b * T:(b + 1) * T], writes=(f"vt{j}",))
                sc.load(rt[j][:], d["rT_f32"][:, c, b * T:(b + 1) * T], writes=(f"rt{j}",))
                A = lambda fn, reads, writes: sc.add("v", fn, reads=reads, writes=writes)
                S = lambda fn, reads, writes: sc.add("s", fn, reads=reads, writes=writes)
                A(lambda e, j=j: e.tensor_scalar(e1[:], kt[j][:], 20.0, -20.0, ALU.min, ALU.max),
                  (f"kt{j}",), ("e1",))
                S(lambda e: e.activation(e1[:], e1[:], AF.Exp), ("e1",), ("e1",))
                A(lambda e, j=j, c=c: e.tensor_scalar(ekt[:], kt[j][:], uu[:, c:c + 1], 20.0, ALU.add, ALU.min),
                  (f"kt{j}", "uu"), ("ekt",))
                A(lambda e: e.tensor_scalar(ekt[:], ekt[:], -20.0, None, ALU.max),
                  ("ekt",), ("ekt",))
                S(lambda e: e.activation(ekt[:], ekt[:], AF.Exp), ("ekt",), ("ekt",))
                A(lambda e, j=j: e.tensor_mul(ekv[:], e1[:], vt[j][:]), ("e1", f"vt{j}"), ("ekv",))
                A(lambda e, j=j: e.tensor_mul(ektv[:], ekt[:], vt[j][:]), ("ekt", f"vt{j}"), ("ektv",))
                A(lambda e, c=c: e.tensor_tensor_scan(At[:], ew[:, c:c + 1].broadcast_to([128, T]),
                                                      ekv[:], 0.0, ALU.mult, ALU.add),
                  ("ekv", "ew"), ("At",))
                A(lambda e, c=c: e.tensor_tensor_scan(Bt[:], ew[:, c:c + 1].broadcast_to([128, T]),
                                                      e1[:], 0.0, ALU.mult, ALU.add),
                  ("e1", "ew"), ("Bt",))
                A(lambda e: e.tensor_tensor(ektv[:, 1:T], At[:, 0:T - 1], ektv[:, 1:T], ALU.add),
                  ("At", "ektv"), ("ektv",))
                A(lambda e: e.tensor_scalar(den[:, 0:1], ekt[:, 0:1], 1e-8, None, ALU.add),
                  ("ekt",), ("den",))
                A(lambda e: e.tensor_tensor(den[:, 1:T], Bt[:, 0:T - 1], ekt[:, 1:T], ALU.add),
                  ("Bt", "ekt"), ("den",))
                A(lambda e: e.tensor_scalar(den[:, 1:T], den[:, 1:T], 1e-8, None, ALU.add),
                  ("den",), ("den",))
                A(lambda e: e.reciprocal(den[:], den[:]), ("den",), ("den",))
                A(lambda e: e.tensor_mul(ektv[:], ektv[:], den[:]), ("ektv", "den"), ("ektv",))
                A(lambda e, j=j, c=c, b=b: e.tensor_mul(rw[:, c, b * T:(b + 1) * T],
                                                        ektv[:], rt[j][:]),
                  ("ektv", f"rt{j}"), (f"rw{c}_{b}",))
            sc.fence_all()

        with ExitStack() as att_scope:
            sb3 = lambda n, s, dt: att_scope.enter_context(nc.sbuf_tensor(nm + n, s, dt))
            xnt = [sb3(f"xnt{j}", [128, C], F32) for j in range(2)]
            ot = [sb3(f"ot{j}", [128, C], F32) for j in range(2)]
            po = [pb(f"po{j}", [128, 1024]) for j in range(2)]
            for i in range(NTT):
                j = i % 2
                sc.load(xnt[j][:], d["xn_f32"][i * 128:(i + 1) * 128, :], writes=(f"xnt{j}",))

                def mmf(e, i=i, j=j):
                    for ci in range(NCH):
                        for h in range(2):
                            mm = e.matmul(po[j][:, h * 512:(h + 1) * 512],
                                          rw[:, ci, i * 128:(i + 1) * 128],
                                          Wo[:, ci, h * 512:(h + 1) * 512],
                                          start=(ci == 0), stop=(ci == NCH - 1))
                    return mm
                rwres = tuple(f"rw{ci}_{bb}" for ci in range(NCH) for bb in range(B))
                sc.add("t", mmf, reads=rwres + tuple(f"Wo{ci}" for ci in range(NCH)),
                       writes=(f"po{j}",))
                sc.add("v", lambda e, j=j: e.tensor_tensor(ot[j][:], po[j][:], xnt[j][:], ALU.add),
                       reads=(f"po{j}", f"xnt{j}"), writes=(f"ot{j}",))
                sc.store(d["x1_f32"][i * 128:(i + 1) * 128, :], ot[j][:], reads=(f"ot{j}",))
        sc.flush()


def build_moe(ctx, d):
    nc = ctx.nc
    nm = ctx.name("moe")
    sc = Sched(ctx, nm)
    NT5 = 512
    with ExitStack() as ses:
        sb = lambda n, s, dt: ses.enter_context(nc.sbuf_tensor(nm + n, s, dt))
        pb = lambda n, s, dt=F32: ses.enter_context(nc.psum_tensor(nm + n, s, dt))
        Wr = sb("Wr", [128, NCH, NE], BF16)
        Ws1 = sb("Ws1", [128, NCH, ED], BF16)
        Ws2 = sb("Ws2", [128, ED // 128, C], BF16)
        W1 = sb("W1", [128, NCH, NE * ED], BF16)
        W2 = sb("W2", [128, ED // 128, NE * C], BF16)
        xnt = [sb(f"xnt{j}", [128, NCH, NT5], BF16) for j in range(2)]
        s1 = sb("s1", [128, ED // 128, NT5], BF16)
        he = sb("he", [128, ED // 128, NT5], BF16)
        relu_t = sb("relu", [128, NT5], F32)
        lg = sb("lg", [128, NE], F32)
        ra = sb("ra", [128, 8], F32)
        prob = sb("prob", [128, NTT, NE], F32)
        f0 = [sb(f"f0_{t}", [128, C], F32) for t in range(4)]
        pss = [pb(f"pss{j}", [128, NT5]) for j in range(2)]
        psy = [pb(f"psy{j}", [128, C]) for j in range(2)]
        pslg = pb("pslg", [128, NE])

        for ci in range(NCH):
            sc.load(Wr[:, ci, :], d["Wr"][ci * 128:(ci + 1) * 128, :], writes=("Wr",))
            sc.load(Ws1[:, ci, :], d["Ws1"][ci * 128:(ci + 1) * 128, :], writes=("Ws1",))
            for e in range(NE):
                sc.load(W1[:, ci, e * ED:(e + 1) * ED], d["W1"][e, ci * 128:(ci + 1) * 128, :],
                        writes=("W1",))
        for edc in range(ED // 128):
            sc.load(Ws2[:, edc, :], d["Ws2"][edc * 128:(edc + 1) * 128, :], writes=("Ws2",))
            for e in range(NE):
                sc.load(W2[:, edc, e * C:(e + 1) * C], d["W2"][e, edc * 128:(edc + 1) * 128, :],
                        writes=("W2",))

        for i in range(NTT):
            j = i % 2
            sc.load(xnt[j][:, :, 0:128], d["xnT_bf"][:, :, i * 128:(i + 1) * 128],
                    writes=(f"xq{j}",))

            def mmlg(e, j=j):
                for ci in range(NCH):
                    mm = e.matmul(pslg[:], xnt[j][:, ci, 0:128], Wr[:, ci, :],
                                  start=(ci == 0), stop=(ci == NCH - 1))
                return mm
            sc.add("t", mmlg, reads=(f"xq{j}", "Wr"), writes=("pslg",))
            sc.add("v", lambda e: e.tensor_copy(lg[:], pslg[:]), reads=("pslg",), writes=("lg",))
            sc.add("v", lambda e: e.reduce_max(ra[:, 0:1], lg[:], axis=AX.X), reads=("lg",), writes=("ra",))
            sc.add("v", lambda e, i=i: e.tensor_scalar(prob[:, i, :], lg[:], ra[:, 0:1], None, ALU.is_equal),
                   reads=("lg", "ra"), writes=(f"pr{i}",))
            sc.add("v", lambda e, i=i: e.reduce_sum(ra[:, 1:2], prob[:, i, :], axis=AX.X),
                   reads=(f"pr{i}",), writes=("ra",))
            sc.add("v", lambda e, i=i: e.scalar_tensor_tensor(lg[:], prob[:, i, :], -1e30, lg[:], ALU.mult, ALU.add),
                   reads=(f"pr{i}", "lg"), writes=("lg",))
            sc.add("v", lambda e: e.reduce_max(ra[:, 2:3], lg[:], axis=AX.X), reads=("lg",), writes=("ra",))
            sc.add("v", lambda e: e.tensor_tensor(ra[:, 3:4], ra[:, 2:3], ra[:, 0:1], ALU.subtract),
                   reads=("ra",), writes=("ra",))
            sc.add("s", lambda e: e.activation(ra[:, 4:5], ra[:, 3:4], AF.Exp), reads=("ra",), writes=("ra",))
            sc.add("v", lambda e: e.tensor_scalar(ra[:, 4:5], ra[:, 4:5], 1.0, None, ALU.add),
                   reads=("ra",), writes=("ra",))
            sc.add("v", lambda e: e.reciprocal(ra[:, 4:5], ra[:, 4:5]), reads=("ra",), writes=("ra",))
            sc.add("v", lambda e: e.tensor_scalar(ra[:, 5:6], ra[:, 4:5], 1.0, None, ALU.is_ge),
                   reads=("ra",), writes=("ra",))
            sc.add("v", lambda e: e.tensor_scalar(ra[:, 6:7], ra[:, 1:2], 1.0, None, ALU.is_le),
                   reads=("ra",), writes=("ra",))
            sc.add("v", lambda e: e.tensor_mul(ra[:, 5:6], ra[:, 5:6], ra[:, 6:7]),
                   reads=("ra",), writes=("ra",))
            sc.add("v", lambda e, i=i: e.tensor_scalar(prob[:, i, :], prob[:, i, :], ra[:, 5:6], None, ALU.mult),
                   reads=(f"pr{i}", "ra"), writes=(f"pr{i}",))

        for n in range(NTOK // NT5):
            j = n % 2
            sc.load(xnt[j][:], d["xnT_bf"][:, :, n * NT5:(n + 1) * NT5], writes=(f"xq{j}",))
            for edc in range(ED // 128):
                jp = edc % 2

                def mms1(e, edc=edc, j=j, jp=jp):
                    for ci in range(NCH):
                        mm = e.matmul(pss[jp][:], Ws1[:, ci, edc * 128:(edc + 1) * 128],
                                      xnt[j][:, ci, :], start=(ci == 0), stop=(ci == NCH - 1))
                    return mm
                sc.add("t", mms1, reads=(f"xq{j}", "Ws1"), writes=(f"pss{jp}",))
                sc.add("s", lambda e, jp=jp: e.activation(relu_t[:], pss[jp][:], AF.Relu),
                       reads=(f"pss{jp}",), writes=("relu",))
                sc.add("v", lambda e, edc=edc: e.tensor_mul(s1[:, edc, :], relu_t[:], relu_t[:]),
                       reads=("relu",), writes=(f"s1_{edc}",))
            for t in range(4):
                jp = t % 2

                def mmsh(e, t=t, jp=jp):
                    for edc in range(ED // 128):
                        for h in range(2):
                            mm = e.matmul(psy[jp][:, h * 512:(h + 1) * 512],
                                          s1[:, edc, t * 128:(t + 1) * 128],
                                          Ws2[:, edc, h * 512:(h + 1) * 512],
                                          start=(edc == 0), stop=(edc == ED // 128 - 1))
                    return mm
                sc.add("t", mmsh, reads=tuple(f"s1_{e2}" for e2 in range(4)) + ("Ws2",),
                       writes=(f"psy{jp}",))
                sc.add("v", lambda e, t=t, jp=jp: e.tensor_copy(f0[t][:], psy[jp][:]),
                       reads=(f"psy{jp}",), writes=(f"f0_{t}",))
            for ex in range(NE):
                for edc in range(ED // 128):
                    jp = edc % 2

                    def mmhe(e, ex=ex, edc=edc, j=j, jp=jp):
                        for ci in range(NCH):
                            mm = e.matmul(pss[jp][:], W1[:, ci, ex * ED + edc * 128: ex * ED + (edc + 1) * 128],
                                          xnt[j][:, ci, :], start=(ci == 0), stop=(ci == NCH - 1))
                        return mm
                    sc.add("t", mmhe, reads=(f"xq{j}", "W1"), writes=(f"pss{jp}",))
                    sc.add("s", lambda e, jp=jp: e.activation(relu_t[:], pss[jp][:], AF.Relu),
                           reads=(f"pss{jp}",), writes=("relu",))
                    sc.add("v", lambda e, edc=edc: e.tensor_mul(he[:, edc, :], relu_t[:], relu_t[:]),
                           reads=("relu",), writes=(f"he_{edc}",))
                for t in range(4):
                    jp = t % 2
                    ti = n * 4 + t

                    def mmy(e, ex=ex, t=t, jp=jp):
                        for edc in range(ED // 128):
                            for h in range(2):
                                mm = e.matmul(psy[jp][:, h * 512:(h + 1) * 512],
                                              he[:, edc, t * 128:(t + 1) * 128],
                                              W2[:, edc, ex * C + h * 512: ex * C + (h + 1) * 512],
                                              start=(edc == 0), stop=(edc == ED // 128 - 1))
                        return mm
                    sc.add("t", mmy, reads=tuple(f"he_{e2}" for e2 in range(4)) + ("W2",),
                           writes=(f"psy{jp}",))
                    sc.add("v", lambda e, t=t, jp=jp, ti=ti, ex=ex: e.scalar_tensor_tensor(
                        f0[t][:], psy[jp][:], prob[:, ti, ex:ex + 1], f0[t][:], ALU.mult, ALU.add),
                        reads=(f"psy{jp}", f"pr{ti}", f"f0_{t}"), writes=(f"f0_{t}",))
            for t in range(4):
                ti = n * 4 + t
                sc.store(d["f0_f32"][ti * 128:(ti + 1) * 128, :], f0[t][:], reads=(f"f0_{t}",))
        sc.flush()


def build_cmix(ctx, d):
    nc = ctx.nc
    nm = ctx.name("cmx")
    sc = Sched(ctx, nm)
    NHC = DH // 128

    with ExitStack() as ses:
        sb = lambda n, s, dt: ses.enter_context(nc.sbuf_tensor(nm + n, s, dt))
        pb = lambda n, s, dt=BF16: ses.enter_context(nc.psum_tensor(nm + n, s, dt))
        xa = [sb(f"xa{j}", [128, C], F32) for j in range(2)]
        sq = [sb(f"sq{j}", [128, C], F32) for j in range(2)]
        hb = [sb(f"hb{j}", [128, C], BF16) for j in range(2)]
        msum = sb("msum", [128, NTT], F32)
        rstd2 = sb("rstd2", [128, NTT], F32)
        ident = sb("ident", [128, 128], BF16)
        tp = [pb(f"tp{j}", [128, 1024]) for j in range(2)]
        slab = [sb(f"sl{j}", [128, NCH, 512], BF16) for j in range(2)]
        make_identity_bf(nc, sc, ident)
        for i in range(NTT):
            j = i % 2
            sc.load(xa[j][:], d["x1_f32"][i * 128:(i + 1) * 128, :], writes=(f"xa{j}",))
            sc.add("v", lambda e, j=j: e.tensor_mul(sq[j][:], xa[j][:], xa[j][:]),
                   reads=(f"xa{j}",), writes=(f"sq{j}",))
            sc.add("v", lambda e, j=j, i=i: e.reduce_sum(msum[:, i:i + 1], sq[j][:], axis=AX.X),
                   reads=(f"sq{j}",), writes=(f"ms{i}",))
        sc.add("v", lambda e: e.tensor_scalar(msum[:], msum[:], 1.0 / C, 1e-6, ALU.mult, ALU.add),
               reads=tuple(f"ms{i}" for i in range(NTT)), writes=("msum",))
        sc.add("s", lambda e: e.activation(rstd2[:], msum[:], AF.Sqrt),
               reads=("msum",), writes=("rstd2",))
        sc.add("v", lambda e: e.reciprocal(rstd2[:], rstd2[:]), reads=("rstd2",), writes=("rstd2",))
        for i in range(NTT):
            j = i % 2
            sc.load(xa[j][:], d["x1_f32"][i * 128:(i + 1) * 128, :], writes=(f"xa{j}",))
            sc.add("v", lambda e, j=j, i=i: e.tensor_scalar(
                hb[j][:], xa[j][:], rstd2[:, i:i + 1], None, ALU.mult),
                reads=(f"xa{j}", "rstd2"), writes=(f"hb{j}",))
            si = (i // 4) % 2
            for c in range(NCH):
                sc.add("t", lambda e, j=j, c=c: e.transpose(
                    tp[j][:, c * 128:(c + 1) * 128], hb[j][:, c * 128:(c + 1) * 128], ident[:]),
                    reads=(f"hb{j}", "ident"), writes=(f"tp{j}_{c}",))
            sc.add("v", lambda e, j=j, si=si, i=i: e.tensor_copy(
                slab[si][:, :, (i % 4) * 128:(i % 4 + 1) * 128],
                tp[j][:].rearrange("p (c f) -> p c f", c=NCH)),
                reads=tuple(f"tp{j}_{c}" for c in range(NCH)), writes=(f"slabw{si}_{i % 4}",))
            if i % 4 == 3:
                i0 = i - 3
                sc.store(d["h2T_bf"][:, :, i0 * 128:(i0 + 4) * 128], slab[si][:],
                         reads=tuple(f"slabw{si}_{k}" for k in range(4)),
                         writes=(f"dh2T{i0 // 4}",))
        sc.fence_all()

    with ExitStack() as ses:
        sb = lambda n, s, dt: ses.enter_context(nc.sbuf_tensor(nm + "b" + n, s, dt))
        pb = lambda n, s, dt=F32: ses.enter_context(nc.psum_tensor(nm + "b" + n, s, dt))
        ln2 = sb("ln2", [128, NCH], F32)
        tmk2 = sb("tmk2", [128, NCH], F32)
        tmr2 = sb("tmr2", [128, NCH], F32)
        omk2 = sb("omk2", [128, NCH], F32)
        omr2 = sb("omr2", [128, NCH], F32)
        Wk2 = sb("Wk2", [128, NCH, DH], BF16)
        Wr2 = sb("Wr2", [128, NCH, C], BF16)
        hT = sb("hT", [128, T], BF16)
        tmp = sb("tmp", [128, T], F32)
        mk2 = sb("mk2", [128, NCH, T], BF16)
        mr2 = sb("mr2", [128, NCH, T], BF16)
        ps = [pb(f"ps{j}", [128, 512]) for j in range(2)]
        ev = [sb(f"ev{j}", [128, 512], BF16) for j in range(2)]
        relu_t = sb("relu", [128, 512], F32)
        psr = [pb(f"psr{j}", [128, C]) for j in range(2)]
        evr = [sb(f"evr{j}", [128, C], F32) for j in range(2)]
        col_loads(sc, ln2, d["ln2_w"], "ln2c")
        col_loads(sc, tmk2, d["tmk2"], "tmk2c")
        col_loads(sc, tmr2, d["tmr2"], "tmr2c")
        sc.add("v", lambda e: e.tensor_scalar(omk2[:], tmk2[:], -1.0, 1.0, ALU.mult, ALU.add),
               reads=tuple(f"tmk2c{c}" for c in range(NCH)), writes=("omk2",))
        sc.add("v", lambda e: e.tensor_scalar(omr2[:], tmr2[:], -1.0, 1.0, ALU.mult, ALU.add),
               reads=tuple(f"tmr2c{c}" for c in range(NCH)), writes=("omr2",))
        for ci in range(NCH):
            sc.load(Wk2[:, ci, :], d["W_key2"][ci * 128:(ci + 1) * 128, :], writes=(f"Wk2{ci}",))
            sc.add("v", lambda e, ci=ci: e.tensor_scalar(
                Wk2[:, ci, :], Wk2[:, ci, :], ln2[:, ci:ci + 1], None, ALU.mult),
                reads=(f"Wk2{ci}",) + tuple(f"ln2c{c}" for c in range(NCH)), writes=(f"Wk2{ci}",))
            sc.load(Wr2[:, ci, :], d["W_recept2"][ci * 128:(ci + 1) * 128, :], writes=(f"Wr2{ci}",))
            sc.add("v", lambda e, ci=ci: e.tensor_scalar(
                Wr2[:, ci, :], Wr2[:, ci, :], ln2[:, ci:ci + 1], None, ALU.mult),
                reads=(f"Wr2{ci}",) + tuple(f"ln2c{c}" for c in range(NCH)), writes=(f"Wr2{ci}",))
        jj = 0
        for b in range(B):
            for c in range(NCH):
                sc.load(hT[:], d["h2T_bf"][:, c, b * T:(b + 1) * T],
                        reads=(f"dh2T{(b * NCH + c) // ((NCH * B) // 8) if False else 0}",) if False else
                              tuple(f"dh2T{kk}" for kk in range(8)),
                        writes=("hT",))
                for mx, tmc, omc in ((mk2, tmk2, omk2), (mr2, tmr2, omr2)):
                    mxn = "mk2" if mx is mk2 else "mr2"
                    sc.add("v", lambda e, c=c, tmc=tmc: e.tensor_scalar(
                        tmp[:], hT[:], tmc[:, c:c + 1], None, ALU.mult),
                        reads=("hT",), writes=("tmp",))
                    omn = "omk2" if mx is mk2 else "omr2"
                    sc.add("v", lambda e, c=c, omc=omc, mx=mx: e.scalar_tensor_tensor(
                        mx[:, c, 1:T], hT[:, 0:T - 1], omc[:, c:c + 1], tmp[:, 1:T], ALU.mult, ALU.add),
                        reads=("hT", "tmp", omn), writes=(f"{mxn}{c}",))
                    sc.add("v", lambda e, c=c, mx=mx: e.tensor_copy(mx[:, c, 0:1], tmp[:, 0:1]),
                           reads=("tmp",), writes=(f"{mxn}{c}",))
            for co in range(NHC):
                for n in range(T // 512):
                    j = jj % 2
                    jj += 1

                    def mmf(e, co=co, n=n, j=j):
                        for ci in range(NCH):
                            mm = e.matmul(ps[j][:], Wk2[:, ci, co * 128:(co + 1) * 128],
                                          mk2[:, ci, n * 512:(n + 1) * 512],
                                          start=(ci == 0), stop=(ci == NCH - 1))
                        return mm
                    sc.add("t", mmf, reads=tuple(f"Wk2{ci}" for ci in range(NCH)) +
                           tuple(f"mk2{ci}" for ci in range(NCH)), writes=(f"ps{j}",))
                    sc.add("s", lambda e, j=j: e.activation(relu_t[:], ps[j][:], AF.Relu),
                           reads=(f"ps{j}",), writes=("relu",))
                    sc.add("v", lambda e, j=j: e.tensor_mul(ev[j][:], relu_t[:], relu_t[:]),
                           reads=("relu",), writes=(f"ev{j}",))
                    sc.store(d["k2T_bf"][:, co, b * T + n * 512: b * T + (n + 1) * 512], ev[j][:],
                             reads=(f"ev{j}",), writes=(f"dk2_{b}_{co}_{n}",))
            for ti in range(T // 128):
                j = ti % 2
                i = b * (T // 128) + ti

                def mmr(e, ti=ti, j=j):
                    for ci in range(NCH):
                        for h in range(2):
                            mm = e.matmul(psr[j][:, h * 512:(h + 1) * 512],
                                          mr2[:, ci, ti * 128:(ti + 1) * 128],
                                          Wr2[:, ci, h * 512:(h + 1) * 512],
                                          start=(ci == 0), stop=(ci == NCH - 1))
                    return mm
                sc.add("t", mmr, reads=tuple(f"Wr2{ci}" for ci in range(NCH)) +
                       tuple(f"mr2{ci}" for ci in range(NCH)), writes=(f"psr{j}",))
                sc.add("s", lambda e, j=j: e.activation(evr[j][:], psr[j][:], AF.Sigmoid),
                       reads=(f"psr{j}",), writes=(f"evr{j}",))
                sc.store(d["r2_f32"][i * 128:(i + 1) * 128, :], evr[j][:],
                         reads=(f"evr{j}",), writes=(f"dr2_{i}",))
        sc.fence_all()

    with ExitStack() as ses:
        sb = lambda n, s, dt: ses.enter_context(nc.sbuf_tensor(nm + "v" + n, s, dt))
        pb = lambda n, s, dt=F32: ses.enter_context(nc.psum_tensor(nm + "v" + n, s, dt))
        Wv2 = sb("Wv2", [128, NHC, C], BF16)
        k2g = [sb(f"k2g{j}", [128, NHC, 512], BF16) for j in range(2)]
        r2t = [sb(f"r2t{j}", [128, C], F32) for j in range(2)]
        x1t = [sb(f"x1t{j}", [128, C], F32) for j in range(2)]
        ot = [sb(f"ot{j}", [128, C], F32) for j in range(2)]
        psv = [pb(f"psv{j}", [128, C]) for j in range(2)]
        for hc in range(NHC):
            sc.load(Wv2[:, hc, :], d["W_value2"][hc * 128:(hc + 1) * 128, :], writes=("Wv2",))
        for g in range(NTOK // 512):
            jg = g % 2
            b, n4 = g // 4, g % 4
            for cos_ in range(0, NHC, 8):
                sc.load(k2g[jg][:, cos_:cos_ + 8, :],
                        d["k2T_bf"][:, cos_:cos_ + 8, g * 512:(g + 1) * 512],
                        reads=tuple(f"dk2_{b}_{co}_{n4}" for co in range(cos_, cos_ + 8)),
                        writes=(f"k2g{jg}",))
            for t4 in range(4):
                i = g * 4 + t4
                j = i % 2
                sc.load(r2t[j][:], d["r2_f32"][i * 128:(i + 1) * 128, :],
                        reads=(f"dr2_{i}",), writes=(f"r2t{j}",))
                sc.load(x1t[j][:], d["x1_f32"][i * 128:(i + 1) * 128, :], writes=(f"x1t{j}",))

                def mmv(e, t4=t4, j=j, jg=jg):
                    for hc in range(NHC):
                        for h in range(2):
                            mm = e.matmul(psv[j][:, h * 512:(h + 1) * 512],
                                          k2g[jg][:, hc, t4 * 128:(t4 + 1) * 128],
                                          Wv2[:, hc, h * 512:(h + 1) * 512],
                                          start=(hc == 0), stop=(hc == NHC - 1))
                    return mm
                sc.add("t", mmv, reads=(f"k2g{jg}", "Wv2"), writes=(f"psv{j}",))
                sc.add("v", lambda e, j=j: e.tensor_tensor(ot[j][:], r2t[j][:], psv[j][:], ALU.mult),
                       reads=(f"r2t{j}", f"psv{j}"), writes=(f"ot{j}",))
                sc.add("v", lambda e, j=j: e.tensor_tensor(ot[j][:], ot[j][:], x1t[j][:], ALU.add),
                       reads=(f"ot{j}", f"x1t{j}"), writes=(f"ot{j}",))
                sc.store(d["xb0a_f32"][i * 128:(i + 1) * 128, :], ot[j][:], reads=(f"ot{j}",))
        sc.flush()


def build_close0_ln1(ctx, d):
    nc = ctx.nc
    nm = ctx.name("cl0")
    sc = Sched(ctx, nm)
    with ExitStack() as ses:
        sb = lambda n, s, dt: ses.enter_context(nc.sbuf_tensor(nm + n, s, dt))
        xt = [sb(f"xt{j}", [128, C], F32) for j in range(2)]
        at = [sb(f"at{j}", [128, C], F32) for j in range(2)]
        ft = [sb(f"ft{j}", [128, C], F32) for j in range(2)]
        for i in range(NTT):
            j = i % 2
            sc.load(xt[j][:], d["x"][i * 128:(i + 1) * 128, :], writes=(f"xt{j}",))
            sc.load(at[j][:], d["xb0a_f32"][i * 128:(i + 1) * 128, :], writes=(f"at{j}",))
            sc.load(ft[j][:], d["f0_f32"][i * 128:(i + 1) * 128, :], writes=(f"ft{j}",))
            sc.add("v", lambda e, j=j: e.tensor_tensor(xt[j][:], xt[j][:], at[j][:], ALU.add),
                   reads=(f"xt{j}", f"at{j}"), writes=(f"xt{j}",))
            sc.add("v", lambda e, j=j: e.tensor_tensor(xt[j][:], xt[j][:], ft[j][:], ALU.add),
                   reads=(f"xt{j}", f"ft{j}"), writes=(f"xt{j}",))
            sc.store(d["out0_f32"][i * 128:(i + 1) * 128, :], xt[j][:], reads=(f"xt{j}",))
        sc.flush()
    build_ln_section(ctx, d, "out0_f32", None, "xn1T_bf", None, "ln1")


def weave(sc, lists, weights):
    idx = [0] * len(lists)
    while any(idx[i] < len(lists[i]) for i in range(len(lists))):
        for li, wt in enumerate(weights):
            for _ in range(wt):
                if idx[li] < len(lists[li]):
                    lists[li][idx[li]]()
                    idx[li] += 1


def build_L1a(ctx, d):
    nc = ctx.nc
    nm = ctx.name("l1a")
    sc = Sched(ctx, nm)
    NLC = LAT // 128
    with ExitStack() as ses:
        sb = lambda n, s, dt: ses.enter_context(nc.sbuf_tensor(nm + n, s, dt))
        pb = lambda n, s, dt=F32: ses.enter_context(nc.psum_tensor(nm + n, s, dt))
        xf = sb("xf", [128, NCH, NTOK], BF16)
        wq = sb("wq", [128, NCH, C], BF16)
        wka = sb("wka", [128, NCH, LAT], BF16)
        wva = sb("wva", [128, NCH, LAT], BF16)
        wkb = sb("wkb", [128, NLC, C], BF16)
        wvb = sb("wvb", [128, NLC, C], BF16)
        ident = sb("ident", [128, 128], BF16)
        kc = sb("kc", [128, NLC, NTOK], BF16)
        vc = sb("vc", [128, NLC, NTOK], BF16)
        crt0 = sb("crt0", [128, 128], F32)
        srt0 = sb("srt0", [128, 128], F32)
        crt = [crt0, crt0]
        srt = [srt0, srt0]
        qb = [sb(f"qb{j}", [128, C], BF16) for j in range(2)]
        kb = [sb(f"kb{j}", [128, C], BF16) for j in range(2)]
        vb = [sb(f"vb{j}", [128, C], BF16) for j in range(2)]
        ta = sb("ta", [128, 128], F32)
        tb = sb("tb", [128, 128], F32)
        tb2 = sb("tb2", [128, 128], F32)
        slab0 = sb("sl0", [128, NCH, 512], BF16)
        slab20 = sb("sl20", [128, NCH, 512], BF16)
        slab = [slab0, slab0]
        slab2 = [slab20, slab20]
        pq = pb("pq", [128, C])
        pk = pb("pk", [128, C])
        pv = pk
        tp = pb("tp", [128, 1024], BF16)
        pkc = pb("pkc", [128, 512])
        w1g = [sb(f"w1g{j}", [128, NCH, 512], BF16) for j in range(2)]
        w2g = [sb(f"w2g{j}", [128, NCH, 512], BF16) for j in range(2)]
        s1c = sb("s1c", [128, DH // 128], F32)
        s2c = sb("s2c", [128, DH // 128], F32)
        sil = sb("sil", [128, 512], F32)
        hv = [sb(f"hv{j}", [128, 512], BF16) for j in range(2)]
        psA0 = pb("psA0", [128, 512])
        psB0 = pb("psB0", [128, 512])
        psA = [psA0, psA0]
        psB = [psB0, psB0]

        make_identity_bf(nc, sc, ident)
        for n in range(NTOK // 512):
            sc.load(xf[:, :, n * 512:(n + 1) * 512], d["xn1T_bf"][:, :, n * 512:(n + 1) * 512],
                    writes=(f"xf{n}",))
        for ci in range(NCH):
            sc.load(wq[:, ci, :], d["wqT"][ci * 128:(ci + 1) * 128, :], writes=("wq",))
            sc.load(wka[:, ci, :], d["wk_aT"][ci * 128:(ci + 1) * 128, :], writes=("wka",))
            sc.load(wva[:, ci, :], d["wv_aT"][ci * 128:(ci + 1) * 128, :], writes=("wva",))
        for lc in range(NLC):
            sc.load(wkb[:, lc, :], d["wk_bT"][lc * 128:(lc + 1) * 128, :], writes=("wkb",))
            sc.load(wvb[:, lc, :], d["wv_bT"][lc * 128:(lc + 1) * 128, :], writes=("wvb",))
        col_loads(sc, s1c, d["w1s"], "s1c")
        col_loads(sc, s2c, d["w2s"], "s2c")

        qkv_jobs = []
        for n in range(NTOK // 512):
            def latjob(n=n):
                for lc in range(NLC):
                    for wsb, dst, rn in ((wka, kc, "kc"), (wva, vc, "vc")):
                        def mmc(e, lc=lc, wsb=wsb, n=n):
                            for ci in range(NCH):
                                mm = e.matmul(pkc[:], wsb[:, ci, lc * 128:(lc + 1) * 128],
                                              xf[:, ci, n * 512:(n + 1) * 512],
                                              start=(ci == 0), stop=(ci == NCH - 1))
                            return mm
                        sc.add("t", mmc, reads=(f"xf{n}", "wka", "wva"), writes=("pkc",))
                        sc.add("v", lambda e, lc=lc, n=n, dst=dst: e.tensor_copy(
                            dst[:, lc, n * 512:(n + 1) * 512], pkc[:]),
                            reads=("pkc",), writes=(f"{rn}{n}",))
            qkv_jobs.append(latjob)
        for i in range(NTT):
            def tilejob(i=i):
                j = i % 2
                pos = (i % (T // 128)) * 128
                n = i // 4
                sc.load(crt[j][:], d["crep"][pos:pos + 128, :], writes=("crtr",))
                sc.load(srt[j][:], d["srep"][pos:pos + 128, :], writes=("srtr",))

                def mmq(e, i=i):
                    for ci in range(NCH):
                        for h in range(2):
                            mm = e.matmul(pq[:, h * 512:(h + 1) * 512],
                                          xf[:, ci, i * 128:(i + 1) * 128],
                                          wq[:, ci, h * 512:(h + 1) * 512],
                                          start=(ci == 0), stop=(ci == NCH - 1))
                    return mm
                sc.add("t", mmq, reads=(f"xf{i // 4}", "wq"), writes=("pq",))

                def mmv(e, i=i):
                    for lc in range(NLC):
                        for h in range(2):
                            mm = e.matmul(pv[:, h * 512:(h + 1) * 512], vc[:, lc, i * 128:(i + 1) * 128],
                                          wvb[:, lc, h * 512:(h + 1) * 512],
                                          start=(lc == 0), stop=(lc == NLC - 1))
                    return mm
                sc.add("t", mmv, reads=(f"vc{n}", "wvb"), writes=("pk",))
                sc.add("v", lambda e, j=j: e.tensor_copy(vb[j][:], pv[:]), reads=("pk",), writes=(f"vb{j}",))
                sc.store(d["v_bf"][i * 128:(i + 1) * 128, :], vb[j][:], reads=(f"vb{j}",))

                def mmk(e, i=i):
                    for lc in range(NLC):
                        for h in range(2):
                            mm = e.matmul(pk[:, h * 512:(h + 1) * 512], kc[:, lc, i * 128:(i + 1) * 128],
                                          wkb[:, lc, h * 512:(h + 1) * 512],
                                          start=(lc == 0), stop=(lc == NLC - 1))
                    return mm
                sc.add("t", mmk, reads=(f"kc{n}", "wkb"), writes=("pk",))
                for (psrc, obuf, obn) in ((pq, qb, "qb"), (pk, kb, "kb")):
                    pn = "pq" if psrc is pq else "pk"
                    rope3 = lambda t_: t_.rearrange("p (nh hd) -> p nh hd", nh=NH)
                    cr3 = lambda t_: t_.rearrange("p (nh e) -> p nh e", nh=NH)
                    sc.add("v", lambda e, psrc=psrc, obuf=obuf, j=j: e.tensor_copy(obuf[j][:], psrc[:]),
                           reads=(pn,), writes=(f"{obn}{j}",))
                    sc.add("v", lambda e, psrc=psrc, j=j, rope3=rope3, cr3=cr3: e.tensor_mul(
                        cr3(ta[:])[:, :, 0:8], rope3(psrc[:])[:, :, 0:8], cr3(crt[j][:])[:, :, 0:8]),
                        reads=(pn, "crtr"), writes=("ta",))
                    sc.add("v", lambda e, psrc=psrc, j=j, rope3=rope3, cr3=cr3: e.tensor_mul(
                        cr3(tb[:])[:, :, 0:8], rope3(psrc[:])[:, :, 8:16], cr3(srt[j][:])[:, :, 0:8]),
                        reads=(pn, "srtr"), writes=("tb",))
                    sc.add("v", lambda e, cr3=cr3: e.tensor_tensor(
                        cr3(ta[:])[:, :, 0:8], cr3(ta[:])[:, :, 0:8], cr3(tb[:])[:, :, 0:8], ALU.subtract),
                        reads=("ta", "tb"), writes=("ta",))
                    sc.add("v", lambda e, psrc=psrc, j=j, rope3=rope3, cr3=cr3: e.tensor_mul(
                        cr3(tb[:])[:, :, 0:8], rope3(psrc[:])[:, :, 0:8], cr3(srt[j][:])[:, :, 0:8]),
                        reads=(pn, "srtr"), writes=("tb",))
                    sc.add("v", lambda e, psrc=psrc, j=j, rope3=rope3, cr3=cr3: e.tensor_mul(
                        cr3(tb2[:])[:, :, 0:8], rope3(psrc[:])[:, :, 8:16], cr3(crt[j][:])[:, :, 0:8]),
                        reads=(pn, "crtr"), writes=("tb2",))
                    sc.add("v", lambda e, obuf=obuf, j=j, rope3=rope3, cr3=cr3: e.tensor_tensor(
                        rope3(obuf[j][:])[:, :, 8:16], cr3(tb[:])[:, :, 0:8], cr3(tb2[:])[:, :, 0:8], ALU.add),
                        reads=("tb", "tb2"), writes=(f"{obn}{j}",))
                    sc.add("v", lambda e, obuf=obuf, j=j, rope3=rope3, cr3=cr3: e.tensor_copy(
                        rope3(obuf[j][:])[:, :, 0:8], cr3(ta[:])[:, :, 0:8]),
                        reads=("ta",), writes=(f"{obn}{j}",))
                si = (i // 4) % 2
                for (obuf, obn, slb, dn, sn) in ((qb, "qb", slab, "qT_bf", "sl"),
                                                 (kb, "kb", slab2, "kTq_bf", "sl2")):
                    for c in range(NCH):
                        sc.add("t", lambda e, obuf=obuf, j=j, c=c: e.transpose(
                            tp[:, c * 128:(c + 1) * 128], obuf[j][:, c * 128:(c + 1) * 128], ident[:]),
                            reads=(f"{obn}{j}", "ident"), writes=(f"tp{c}",))
                    sc.add("v", lambda e, slb=slb, si=si, i=i: e.tensor_copy(
                        slb[si][:, :, (i % 4) * 128:(i % 4 + 1) * 128],
                        tp[:].rearrange("p (c f) -> p c f", c=NCH)),
                        reads=tuple(f"tp{c}" for c in range(NCH)), writes=(f"{sn}w{si}_{i % 4}",))
                    if i % 4 == 3:
                        i0 = i - 3
                        sc.store(d[dn][:, :, i0 * 128:(i0 + 4) * 128], slb[si][:],
                                 reads=tuple(f"{sn}w{si}_{k}" for k in range(4)))
            qkv_jobs.append(tilejob)

        ab_jobs = []
        w13 = d["w1T"].rearrange("(cb p) o -> p cb o", p=128)
        w23 = d["w2T"].rearrange("(cb p) o -> p cb o", p=128)
        for hg in range(DH // 512):
            def wload(hg=hg):
                jw = hg % 2
                sc.load(w1g[jw][:], w13[:, :, hg * 512:(hg + 1) * 512], writes=(f"w1g{jw}",))
                sc.load(w2g[jw][:], w23[:, :, hg * 512:(hg + 1) * 512], writes=(f"w2g{jw}",))
            ab_jobs.append(wload)
            for hc4 in range(4):
                for n in range(NTOK // 512):
                    def abjob(hg=hg, hc4=hc4, n=n):
                        jw = hg % 2
                        hc = hg * 4 + hc4
                        j = n % 2

                        def mma(e, hc4=hc4, n=n, j=j, jw=jw):
                            for ci in range(NCH):
                                mm = e.matmul(psA[j][:], w1g[jw][:, ci, hc4 * 128:(hc4 + 1) * 128],
                                              xf[:, ci, n * 512:(n + 1) * 512],
                                              start=(ci == 0), stop=(ci == NCH - 1))
                            return mm
                        sc.add("t", mma, reads=(f"w1g{jw}", f"xf{n}"), writes=("psA0",))

                        def mmb(e, hc4=hc4, n=n, j=j, jw=jw):
                            for ci in range(NCH):
                                mm = e.matmul(psB[j][:], w2g[jw][:, ci, hc4 * 128:(hc4 + 1) * 128],
                                              xf[:, ci, n * 512:(n + 1) * 512],
                                              start=(ci == 0), stop=(ci == NCH - 1))
                            return mm
                        sc.add("t", mmb, reads=(f"w2g{jw}", f"xf{n}"), writes=("psB0",))
                        sc.add("s", lambda e, j=j, hc=hc: e.activation(
                            sil[:], psA[j][:], AF.Silu, scale=s1c[:, hc:hc + 1]),
                            reads=("psA0", "s1c"), writes=("sil",))
                        sc.add("v", lambda e, j=j, hc=hc: e.scalar_tensor_tensor(
                            hv[j][:], psB[j][:], s2c[:, hc:hc + 1], sil[:], ALU.mult, ALU.mult),
                            reads=("psB0", "s2c", "sil"), writes=(f"hv{j}",))
                        sc.store(d["hfb_bf"][:, hc, n * 512:(n + 1) * 512], hv[j][:],
                                 reads=(f"hv{j}",), writes=(f"dh_{hc}_{n}",))
                    ab_jobs.append(abjob)
        weave(sc, [qkv_jobs, ab_jobs], [1, 7])
        sc.flush()


def build_L1b(ctx, d):
    nc = ctx.nc
    nm = ctx.name("l1b")
    sc = Sched(ctx, nm)
    NHC = DH // 128
    with ExitStack() as ses:
        sb = lambda n, s, dt: ses.enter_context(nc.sbuf_tensor(nm + n, s, dt))
        pb = lambda n, s, dt=F32: ses.enter_context(nc.psum_tensor(nm + n, s, dt))
        qh0 = sb("qh0", [128, T], BF16)
        kh0 = sb("kh0", [128, T], BF16)
        qh = [qh0, qh0]
        kh = [kh0, kh0]
        vsb = sb("vsb", [128, T // 128, C], BF16)
        oh = [sb(f"oh{j}", [128, T], BF16) for j in range(2)]
        Pt2 = [[sb(f"Pt{j}_{hd_}", [128, T], BF16) for hd_ in range(2)] for j in range(2)]
        PTt2 = [[sb(f"PTt{j}", [128, T // 2], BF16)] for j in range(2)]
        stt_ = [sb(f"stt{j}", [128, 16], F32) for j in range(2)]
        S4 = pb("S4", [128, T])
        psO = pb("psO", [64, T // 2])
        hn = [sb(f"hn{j}", [128, NHC, 512], BF16) for j in range(2)]
        wq_ = sb("wqh", [128, NHC, 1024], BF16)
        ssc = sb("ssc", [128, 2, NHC], F32)
        gv = [sb(f"gv{j}", [128, 512], BF16) for j in range(2)]
        ps = [pb(f"ps{j}", [128, 512]) for j in range(2)]

        col_loads(sc, ssc[:, 0, :], d["h1s"], "ssc1_")
        col_loads(sc, ssc[:, 1, :], d["h2s"], "ssc2_")

        attn_jobs = []
        PD = {(0, 0): "P_bf", (0, 1): "P2_bf", (1, 0): "P3_bf", (1, 1): "P4_bf"}

        def emit_loads(b, m):
            jh = m % 2
            sc.load(qh[jh][:], d["qT_bf"][:, m, b * T:(b + 1) * T], writes=("qhr",))
            sc.load(kh[jh][:], d["kTq_bf"][:, m, b * T:(b + 1) * T], writes=("khr",))

        def emit_vload(b):
            for t in range(0, T // 128, 4):
                sc.load(vsb[:, t:t + 4, :],
                        d["v_bf"][(b * T // 128 + t) * 128:(b * T // 128 + t + 4) * 128, :]
                        .rearrange("(t p) c -> p t c", p=128),
                        writes=("vsb",))

        def emit_score_tile(b, m, tq):
            jh = m % 2
            j = tq % 2
            st_ = stt_[j]
            par = m % 2
            for half in range(2):
                for hd_ in range(2):
                    hr = f"S4h{hd_}"
                    hp = hd_ * 64
                    pbase = hd_ * 1024
                    for tkc in (0, 1):
                        def mms(e, jh=jh, tq=tq, half=half, tkc=tkc, hp=hp, pbase=pbase):
                            return e.matmul(S4[:, pbase + tkc * 512: pbase + (tkc + 1) * 512],
                                            qh[jh][hp:hp + 64, tq * 128:(tq + 1) * 128],
                                            kh[jh][hp:hp + 64, half * 1024 + tkc * 512: half * 1024 + (tkc + 1) * 512],
                                            start=True, stop=True,
                                            tile_position=(hp, 0))
                        sc.add("t", mms, reads=("qhr", "khr"), writes=(hr,))
                for hd_ in range(2):
                    hr = f"S4h{hd_}"
                    pbase = hd_ * 1024
                    ho = hd_ * 8
                    sc.add("v", lambda e, j=j, half=half, st_=st_, pbase=pbase, ho=ho: e.reduce_max(
                        st_[:, ho + half:ho + half + 1], S4[:, pbase:pbase + 1024],
                        axis=AX.X, negate=True),
                        reads=(hr,), writes=(f"st{j}_{hd_}{half}",))
                    sc.add("v", lambda e, st_=st_, ho=ho, half=half: e.tensor_scalar(
                        st_[:, ho + half:ho + half + 1], st_[:, ho + half:ho + half + 1],
                        0.125, None, ALU.mult),
                        reads=(f"st{j}_{hd_}{half}",), writes=(f"st{j}_{hd_}{half}",))
                    sc.add("s", lambda e, j=j, half=half, st_=st_, hd_=hd_, pbase=pbase, ho=ho: e.activation(
                        Pt[j][:, pbase + half * 1024 - hd_ * 1024 + hd_ * 0:0] if False else
                        Pt[j][:, hd_ * 2048 + half * 1024: hd_ * 2048 + (half + 1) * 1024]
                        if False else
                        Pt2[j][hd_][:, half * 1024:(half + 1) * 1024],
                        S4[:, pbase:pbase + 1024],
                        AF.Exp, bias=st_[:, ho + half:ho + half + 1], scale=0.125,
                        accum_out=st_[:, ho + 5 + half:ho + 6 + half]),
                        reads=(hr, f"st{j}_{hd_}{half}"), writes=(f"Pt{j}_{hd_}{half}", f"st{j}_s{hd_}{half}"))
            for hd_ in range(2):
                ho = hd_ * 8
                sc.add("v", lambda e, st_=st_, ho=ho: e.tensor_reduce(
                    st_[:, ho + 2:ho + 3], st_[:, ho:ho + 2], AX.X, ALU.min),
                    reads=(f"st{j}_{hd_}0", f"st{j}_{hd_}1"), writes=(f"st{j}_n{hd_}",))
                sc.add("v", lambda e, st_=st_, ho=ho: e.tensor_scalar(
                    st_[:, ho + 3:ho + 5], st_[:, ho:ho + 2], st_[:, ho + 2:ho + 3], None, ALU.subtract),
                    reads=(f"st{j}_n{hd_}", f"st{j}_{hd_}0", f"st{j}_{hd_}1"), writes=(f"st{j}_d{hd_}",))
                sc.add("s", lambda e, st_=st_, ho=ho: e.activation(
                    st_[:, ho + 3:ho + 5], st_[:, ho + 3:ho + 5], AF.Exp, scale=-1.0),
                    reads=(f"st{j}_d{hd_}",), writes=(f"st{j}_c{hd_}",))
                sc.add("v", lambda e, st_=st_, ho=ho: e.tensor_mul(
                    st_[:, ho + 5:ho + 7], st_[:, ho + 5:ho + 7], st_[:, ho + 3:ho + 5]),
                    reads=(f"st{j}_c{hd_}", f"st{j}_s{hd_}0", f"st{j}_s{hd_}1"), writes=(f"st{j}_sc{hd_}",))
                sc.add("v", lambda e, st_=st_, ho=ho: e.reduce_sum(
                    st_[:, ho + 7:ho + 8], st_[:, ho + 5:ho + 7], axis=AX.X),
                    reads=(f"st{j}_sc{hd_}",), writes=(f"st{j}_sum{hd_}",))
                sc.add("v", lambda e, st_=st_, ho=ho: e.reciprocal(
                    st_[:, ho + 7:ho + 8], st_[:, ho + 7:ho + 8]),
                    reads=(f"st{j}_sum{hd_}",), writes=(f"st{j}_sum{hd_}",))
                sc.add("v", lambda e, st_=st_, ho=ho: e.tensor_scalar(
                    st_[:, ho + 3:ho + 5], st_[:, ho + 3:ho + 5], st_[:, ho + 7:ho + 8], None, ALU.mult),
                    reads=(f"st{j}_c{hd_}", f"st{j}_sum{hd_}"), writes=(f"st{j}_f{hd_}",))
                for half in range(2):
                    sc.add("v", lambda e, j=j, st_=st_, hd_=hd_, half=half, ho=ho: e.tensor_scalar(
                        Pt2[j][hd_][:, half * 1024:(half + 1) * 1024],
                        Pt2[j][hd_][:, half * 1024:(half + 1) * 1024],
                        st_[:, ho + 3 + half:ho + 4 + half], None, ALU.mult),
                        reads=(f"Pt{j}_{hd_}{half}", f"st{j}_f{hd_}"), writes=(f"Pt{j}_{hd_}{half}",))
                sc.store(d[PD[(hd_, par)]][tq * 128:(tq + 1) * 128, :], Pt2[j][hd_][:],
                         reads=(f"Pt{j}_{hd_}0", f"Pt{j}_{hd_}1"),
                         writes=(f"dP{hd_}{par}_{tq}",), q="g")

        def emit_av(b, m, step):
            par = m % 2
            hd_ = step // 32
            half = (step // 16) % 2
            tkc = step % 16
            jx = step % 2
            sc.add("y", lambda e, tkc=tkc, jx=jx, half=half, hd_=hd_, par=par: e.dma_start(
                out=PTt2[jx][0][:],
                in_=d[PD[(hd_, par)]][half * 1024:(half + 1) * 1024, tkc * 128:(tkc + 1) * 128],
                transpose=True),
                reads=tuple(f"dP{hd_}{par}_{tq}" for tq in range(T // 128)),
                writes=(f"PTt{jx}",), dma=True)

            def mmav(e, tkc=tkc, jx=jx, m=m, hd_=hd_):
                for tqc in range(2):
                    mm = e.matmul(psO[:, tqc * 512:(tqc + 1) * 512],
                                  vsb[:, tkc, m * 128 + hd_ * 64: m * 128 + hd_ * 64 + 64],
                                  PTt2[jx][0][:, tqc * 512:(tqc + 1) * 512],
                                  start=(tkc == 0), stop=(tkc == T // 128 - 1))
                return mm
            sc.add("t", mmav, reads=(f"PTt{jx}", "vsb"), writes=("psO",))
            if tkc == T // 128 - 1:
                jo = m % 2
                sc.add("v", lambda e, jo=jo, half=half, hd_=hd_: e.tensor_copy(
                    oh[jo][hd_ * 64:hd_ * 64 + 64, half * 1024:(half + 1) * 1024], psO[:]),
                    reads=("psO",), writes=(f"oh{jo}_{hd_}{half}",))
                if half == 1 and hd_ == 1:
                    sc.store(d["OT_bf"][:, m, b * T:(b + 1) * T], oh[jo][:],
                             reads=(f"oh{jo}_00", f"oh{jo}_01", f"oh{jo}_10", f"oh{jo}_11"), q="g")

        def unit(b, m, i):
            def f():
                if i == 0 and m < NH // 2:
                    emit_loads(b, m)
                if m == 0 and i == 0:
                    emit_vload(b)
                if m < NH // 2:
                    emit_score_tile(b, m, i)
                if m > 0:
                    for k in range(4):
                        emit_av(b, m - 1, 4 * i + k)
            return f

        for b in range(B):
            for m in range(NH // 2 + 1):
                for i in range(T // 128):
                    attn_jobs.append(unit(b, m, i))

        h_jobs = []
        for ph, win, hin, hout, sidx in (("1", "h1T", "hfb_bf", "g1_bf", 0),
                                         ("2", "h2T", "g1_bf", "g2_bf", 1)):
            win3 = d[win].rearrange("(cb p) o -> p cb o", p=128)
            hin_t = d[hin]
            for coq in range(4):
                for hw in range(2):
                    def wload(coq=coq, hw=hw, win3=win3, ph=ph):
                        for cis in range(0, NHC, 8):
                            sc.load(wq_[:, cis:cis + 8, hw * 512:(hw + 1) * 512],
                                    win3[:, cis:cis + 8, coq * 1024 + hw * 512: coq * 1024 + (hw + 1) * 512],
                                    writes=(f"wqh{hw}_{cis}",))
                    h_jobs.append(wload)
                for n in range(NTOK // 512):
                    def hload(n=n, hin_t=hin_t, ph=ph):
                        jn = n % 2
                        for cis in range(0, NHC, 8):
                            sc.load(hn[jn][:, cis:cis + 8, :],
                                    hin_t[:, cis:cis + 8, n * 512:(n + 1) * 512],
                                    reads=tuple((f"dh_{ci}_{n}" if ph == "1" else f"dg1_{ci}_{n}")
                                                for ci in range(cis, cis + 8)),
                                    writes=(f"hn{jn}_{cis}",))
                    h_jobs.append(hload)
                    for co8 in range(8):
                        def hjob(coq=coq, n=n, co8=co8, ph=ph, hout=hout, sidx=sidx):
                            hw, co4 = co8 // 4, co8 % 4
                            jn = n % 2
                            co = coq * 8 + co8
                            j = co8 % 2

                            def mmh(e, hw=hw, co4=co4, j=j, jn=jn):
                                for ci in range(NHC):
                                    mm = e.matmul(ps[j][:],
                                                  wq_[:, ci, hw * 512 + co4 * 128: hw * 512 + (co4 + 1) * 128],
                                                  hn[jn][:, ci, :], start=(ci == 0), stop=(ci == NHC - 1))
                                return mm
                            sc.add("t", mmh,
                                   reads=tuple(f"wqh{hw}_{cis}" for cis in range(0, NHC, 8)) +
                                         tuple(f"hn{jn}_{cis}" for cis in range(0, NHC, 8)),
                                   writes=(f"ps{j}",))
                            sc.add("s", lambda e, j=j, co=co, sidx=sidx: e.activation(
                                gv[j][:], ps[j][:], AF.Gelu, scale=ssc[:, sidx, co:co + 1]),
                                reads=(f"ps{j}", (f"ssc1_{co}" if sidx == 0 else f"ssc2_{co}")),
                                writes=(f"gv{j}",))
                            sc.store(d[hout][:, co, n * 512:(n + 1) * 512], gv[j][:],
                                     reads=(f"gv{j}",),
                                     writes=((f"dg1_{co}_{n}",) if ph == "1" else (f"dg2_{co}_{n}",)))
                        h_jobs.append(hjob)
        weave(sc, [attn_jobs, h_jobs], [8, 17])
        sc.flush()


def build_L1c(ctx, d):
    nc = ctx.nc
    nm = ctx.name("l1c")
    sc = Sched(ctx, nm)
    NHC = DH // 128
    with ExitStack() as ses:
        sb = lambda n, s, dt: ses.enter_context(nc.sbuf_tensor(nm + n, s, dt))
        pb = lambda n, s, dt=F32: ses.enter_context(nc.psum_tensor(nm + n, s, dt))
        wo = sb("wo", [128, NCH, C], BF16)
        ott = [sb(f"ott{j}", [128, NCH, 512], BF16) for j in range(2)]
        evw = [sb(f"evw{j}", [128, C], F32) for j in range(2)]
        psw = [pb(f"psw{j}", [128, C]) for j in range(2)]
        w3 = sb("w3", [128, NHC, C], BF16)
        w3row = sb("w3row", [1, C], F32)
        onesc = sb("onesc", [1, 128], F32)
        wsb = sb("wsb", [128, C], F32)
        g2t = [sb(f"g2t{j}", [128, NHC, 512], BF16) for j in range(2)]
        o0t = [sb(f"o0t{j}", [128, C], F32) for j in range(2)]
        att = [sb(f"att{j}", [128, C], F32) for j in range(2)]
        ov = [sb(f"ov{j}", [128, C], F32) for j in range(2)]
        ps3 = [pb(f"ps3{j}", [128, C]) for j in range(2)]
        psb = psw[0]
        for c in range(NCH):
            sc.load(wo[:, c, :], d["woT"][c * 128:(c + 1) * 128, :], writes=("wo",))
        for ci in range(NHC):
            sc.load(w3[:, ci, :], d["w3T"][ci * 128:(ci + 1) * 128, :], writes=("w3",))
        sc.load(w3row[:], d["w3s"].unsqueeze(0), writes=("w3row",))
        sc.add("g", lambda e: e.memset(onesc[:], 1.0), writes=("onesc",))

        def mmbc(e):
            for hf in range(2):
                mm = e.matmul(psb[:, hf * 512:(hf + 1) * 512], onesc[:],
                              w3row[:, hf * 512:(hf + 1) * 512], start=True, stop=True)
            return mm
        sc.add("t", mmbc, reads=("onesc", "w3row"), writes=("psw0",))
        sc.add("v", lambda e: e.tensor_copy(wsb[:], psb[:]), reads=("psw0",), writes=("wsb",))

        wo_jobs = []
        for g in range(NTOK // 512):
            def wog(g=g):
                jg = g % 2
                sc.load(ott[jg][:], d["OT_bf"][:, :, g * 512:(g + 1) * 512], writes=(f"ott{jg}",))
                for t4 in range(4):
                    i = g * 4 + t4
                    j = i % 2

                    def mmw(e, j=j, jg=jg, t4=t4):
                        for dc in range(NCH):
                            for h in range(2):
                                mm = e.matmul(psw[j][:, h * 512:(h + 1) * 512],
                                              ott[jg][:, dc, t4 * 128:(t4 + 1) * 128],
                                              wo[:, dc, h * 512:(h + 1) * 512],
                                              start=(dc == 0), stop=(dc == NCH - 1))
                        return mm
                    sc.add("t", mmw, reads=(f"ott{jg}", "wo"), writes=(f"psw{j}",))
                    sc.add("v", lambda e, j=j: e.tensor_copy(evw[j][:], psw[j][:]),
                           reads=(f"psw{j}",), writes=(f"evw{j}",))
                    sc.store(d["att_f32"][i * 128:(i + 1) * 128, :], evw[j][:],
                             reads=(f"evw{j}",), writes=(f"datt{i}",))
            wo_jobs.append(wog)

        w3_jobs = []
        for g in range(NTOK // 512):
            def w3g(g=g):
                jg = g % 2
                for cis in range(0, NHC, 8):
                    sc.load(g2t[jg][:, cis:cis + 8, :], d["g2_bf"][:, cis:cis + 8, g * 512:(g + 1) * 512],
                            reads=tuple(f"dg2_{ci}_{g}" for ci in range(cis, cis + 8)),
                            writes=(f"g2t{jg}",))
                for t4 in range(4):
                    i = g * 4 + t4
                    j = i % 2
                    sc.load(o0t[j][:], d["out0_f32"][i * 128:(i + 1) * 128, :], writes=(f"o0t{j}",))
                    sc.load(att[j][:], d["att_f32"][i * 128:(i + 1) * 128, :],
                            reads=(f"datt{i}",), writes=(f"att{j}",))

                    def mmw3(e, t4=t4, j=j, jg=jg):
                        for ci in range(NHC):
                            for hf in range(2):
                                mm = e.matmul(ps3[j][:, hf * 512:(hf + 1) * 512],
                                              g2t[jg][:, ci, t4 * 128:(t4 + 1) * 128],
                                              w3[:, ci, hf * 512:(hf + 1) * 512],
                                              start=(ci == 0), stop=(ci == NHC - 1))
                        return mm
                    sc.add("t", mmw3, reads=(f"g2t{jg}", "w3"), writes=(f"ps3{j}",))
                    sc.add("v", lambda e, j=j: e.tensor_tensor(ov[j][:], ps3[j][:], wsb[:], ALU.mult),
                           reads=(f"ps3{j}", "wsb"), writes=(f"ov{j}",))
                    sc.add("v", lambda e, j=j: e.tensor_tensor(ov[j][:], ov[j][:], o0t[j][:], ALU.add),
                           reads=(f"ov{j}", f"o0t{j}"), writes=(f"ov{j}",))
                    sc.add("v", lambda e, j=j: e.tensor_tensor(ov[j][:], ov[j][:], att[j][:], ALU.add),
                           reads=(f"ov{j}", f"att{j}"), writes=(f"ov{j}",))
                    sc.store(d["out"][i * 128:(i + 1) * 128, :], ov[j][:], reads=(f"ov{j}",))
            w3_jobs.append(w3g)
        weave(sc, [wo_jobs, w3_jobs], [1, 1])
        sc.flush()


def build_all(ctx, d):
    build_ln_section(ctx, d, "x", "xn_f32", "xnT_bf", "hT_bf", "ln0")
    build_kvr(ctx, d)
    build_wkv(ctx, d)
    build_moe(ctx, d)
    build_cmix(ctx, d)
    build_close0_ln1(ctx, d)
    build_L1a(ctx, d)
    build_L1b(ctx, d)
    build_L1c(ctx, d)

from concourse.bass_utils import run_bass_kernel_spmd


def kernel(**inputs):
    w, shards = host_prep(inputs)
    nc = bass.Bass("TRN2", target_bir_lowering=False)
    d = declare_drams(nc, inspect=())
    ctx = Ctx(nc)
    build_all(ctx, d)
    in_maps = []
    for c in range(8):
        m = dict(w)
        m["x"] = shards[c]
        in_maps.append(m)
    res = run_bass_kernel_spmd(nc, in_maps, core_ids=list(range(8)))
    out = np.concatenate([res.results[c]["out"].reshape(2, T, C) for c in range(8)], axis=0)
    return out.astype(np.float32)


# version 21
# speedup vs baseline: 1.0720x; 1.0720x over previous
import os
import numpy as np
import ml_dtypes
import concourse.bass as bass
import concourse.mybir as mybir
from contextlib import ExitStack

SERIALIZE = int(os.environ.get("KSER", "0"))

F32 = mybir.dt.float32
BF16 = mybir.dt.bfloat16
AF = mybir.ActivationFunctionType
ALU = mybir.AluOpType
AX = mybir.AxisListType

B, T, C = 2, 2048, 1024
NTOK = B * T
NCH = C // 128
NTT = NTOK // 128
NH, HD, DROPE, LAT = 16, 64, 16, 256
DH = 4096
ED, NE = 512, 8

ENGS = ("g", "t", "v", "s", "y")


NCHAIN = 4


class Op:
    __slots__ = ("eng", "fn", "idx", "inc_idx", "dma", "needed", "deps", "lane")

    def __init__(self, eng, fn, dma, deps):
        self.eng = eng
        self.fn = fn
        self.dma = dma
        self.deps = deps
        self.idx = -1
        self.inc_idx = -1
        self.needed = dma
        self.lane = None


class Sched:

    def __init__(self, ctx, name):
        self.ctx = ctx
        self.name = name
        self.ops = {e: [] for e in ENGS}
        self.all_ops = []
        self.last_writer = {}
        self.readers = {}
        self.phase_deps = []
        self.dma_count = {}
        self.chain_prev = {}

    def fence_all(self):
        self.phase_deps = [self.ops[e][-1] for e in ENGS if self.ops[e]]

    def add(self, eng, fn, reads=(), writes=(), dma=False, extra_deps=()):
        deps = set()
        if dma:
            nq = self.dma_count.get(eng, 0)
            self.dma_count[eng] = nq + 1
            chain = nq % NCHAIN
            prev = self.chain_prev.get((eng, chain))
            if prev is not None:
                deps.add(prev)
        for r in reads:
            w = self.last_writer.get(r)
            if w is not None:
                deps.add(w)
        for r in writes:
            w = self.last_writer.get(r)
            if w is not None:
                deps.add(w)
            for rd in self.readers.get(r, ()):
                deps.add(rd)
        for d in extra_deps:
            if d is not None:
                deps.add(d)
        for d in self.phase_deps:
            deps.add(d)
        if SERIALIZE == 1 and self.all_ops:
            deps.add(self.all_ops[-1])
        elif SERIALIZE == 2 and self.all_ops:
            if eng == "t" or self.all_ops[-1].eng == "t":
                deps.update(o2 for o2 in self.all_ops[-40:])
        elif SERIALIZE == 3 and self.all_ops:
            if dma or self.all_ops[-1].dma:
                deps.update(o2 for o2 in self.all_ops[-40:])
        elif SERIALIZE == 4 and dma and self.ops[eng]:
            prev = [o2 for o2 in self.ops[eng] if o2.dma]
            if prev:
                deps.add(prev[-1])
        op = Op(eng, fn, dma, deps)
        op.lane = (eng, chain) if dma else (eng,)
        if dma:
            self.chain_prev[(eng, chain)] = op
        op.idx = len(self.ops[eng])
        self.ops[eng].append(op)
        self.all_ops.append(op)
        for r in writes:
            self.last_writer[r] = op
            self.readers[r] = []
        for r in reads:
            self.readers.setdefault(r, []).append(op)
        return op

    def load(self, dst_ap, src_ap, reads=(), writes=(), q="g"):
        return self.add(q, lambda e: e.dma_start(out=dst_ap, in_=src_ap),
                        reads=reads, writes=writes, dma=True)

    def store(self, dst_ap, src_ap, reads=(), writes=(), q="y"):
        return self.add(q, lambda e: e.dma_start(out=dst_ap, in_=src_ap),
                        reads=reads, writes=writes, dma=True)

    def flush(self):
        ctx = self.ctx
        for op in self.all_ops:
            for d in op.deps:
                d.needed = True
        for e in ENGS:
            if self.ops[e]:
                self.ops[e][-1].needed = True
        lane_n = ctx.lane_n
        lanes_used = set()
        for e in ENGS:
            for op in self.ops[e]:
                if op.needed:
                    lane_n[op.lane] = lane_n.get(op.lane, 0) + (16 if op.dma else 1)
                    lanes_used.add(op.lane)
                op.inc_idx = lane_n.get(op.lane, 0)
        totals = {ln: lane_n[ln] for ln in lanes_used}
        sems = {}
        for ln in lanes_used:
            sems[ln] = ctx.lane_sem(ln)
        with ctx.nc.Block() as block:
            eng_of = {"g": block.gpsimd, "t": block.tensor, "v": block.vector,
                      "s": block.scalar, "y": block.sync}
            for e in ENGS:
                ops = self.ops[e]
                if not ops:
                    continue

                def emit(eng, ops=ops, e=e):
                    if ctx.fval > 0:
                        eng.wait_ge(ctx.fence, ctx.fval)
                    waited = {}
                    for op in ops:
                        need = {}
                        for d in op.deps:
                            need[d.lane] = max(need.get(d.lane, 0), d.inc_idx)
                        for key, val in need.items():
                            if waited.get(key, 0) < val:
                                eng.wait_ge(sems[key], val)
                                waited[key] = val
                        inst = op.fn(eng)
                        if op.needed:
                            inst.then_inc(sems[op.lane], 16 if op.dma else 1)

                eng_of[e](emit)

        def closer(eng):
            for ln, tot in totals.items():
                eng.wait_ge(sems[ln], tot)
            eng.nop().then_inc(ctx.fence, 1)

        with ctx.nc.Block() as block2:
            block2.gpsimd(closer)
        ctx.fval += 1


class Ctx:
    def __init__(self, nc):
        self.nc = nc
        self.es = ExitStack()
        self.fence = self.es.enter_context(nc.semaphore("fence"))
        self.fval = 0
        self.uid = 0
        self.lane_n = {}
        self.lane_sems = {}

    def sem(self, name):
        self.uid += 1
        return self.es.enter_context(self.nc.semaphore(f"{name}_{self.uid}"))

    def lane_sem(self, lane):
        if lane not in self.lane_sems:
            self.lane_sems[lane] = self.sem("L" + "".join(str(x) for x in lane))
        return self.lane_sems[lane]

    def name(self, n):
        self.uid += 1
        return f"{n}{self.uid}_"


def _fix_flush(sched):
    sched.flush()



def make_identity_bf(nc, sc, ident):
    def f1(e):
        e.memset(ident[:], 0.0)
        return e.affine_select(out=ident[:], in_=ident[:], compare_op=ALU.not_equal,
                               fill=1.0, base=0, pattern=[[-1, 128]], channel_multiplier=1)
    return sc.add("g", f1, writes=("ident",))


def build_ln_section(ctx, d, src_name, xn_f32_name, xnT_name, hT_name, out_pref):
    nc = ctx.nc
    nm = ctx.name(out_pref)
    sc = Sched(ctx, nm)
    src = d[src_name]
    with ExitStack() as ses:
        sb = lambda n, s, dt: ses.enter_context(nc.sbuf_tensor(nm + n, s, dt))
        pb = lambda n, s, dt=F32: ses.enter_context(nc.psum_tensor(nm + n, s, dt))
        xa = [sb(f"xa{j}", [128, C], F32) for j in range(2)]
        sq = [sb(f"sq{j}", [128, C], F32) for j in range(2)]
        xn_t = [sb(f"xn{j}", [128, C], F32) for j in range(2)]
        xnb = [sb(f"xnb{j}", [128, C], BF16) for j in range(2)]
        hb = [sb(f"hb{j}", [128, C], BF16) for j in range(2)]
        mus = sb("mus", [128, NTT], F32)
        sqs = sb("sqs", [128, NTT], F32)
        rstd = sb("rstd", [128, NTT], F32)
        msum = sb("msum", [128, NTT], F32)
        rstd2 = sb("rstd2", [128, NTT], F32)
        ident = sb("ident", [128, 128], BF16)
        tp = [pb(f"tp{j}", [128, 1024], BF16) for j in range(2)]
        slab = [sb(f"sl{j}", [128, NCH, 512], BF16) for j in range(2)]
        slab2 = [sb(f"sl2{j}", [128, NCH, 512], BF16) for j in range(2)]

        make_identity_bf(nc, sc, ident)

        for i in range(NTT):
            j = i % 2
            ld = sc.load(xa[j][:], src[i * 128:(i + 1) * 128, :],
                         writes=(f"xa{j}",))
            sc.add("v", lambda e, j=j, i=i: e.reduce_sum(mus[:, i:i + 1], xa[j][:], axis=AX.X),
                   reads=(f"xa{j}",), writes=(f"mu{i}",))
            sc.add("v", lambda e, j=j: e.tensor_mul(sq[j][:], xa[j][:], xa[j][:]),
                   reads=(f"xa{j}",), writes=(f"sq{j}",))
            sc.add("v", lambda e, j=j, i=i: e.reduce_sum(sqs[:, i:i + 1], sq[j][:], axis=AX.X),
                   reads=(f"sq{j}", f"xa{j}"), writes=(f"sqs{i}",))
        sc.add("v", lambda e: e.tensor_scalar(mus[:], mus[:], 1.0 / C, None, ALU.mult),
               reads=tuple(f"mu{i}" for i in range(NTT)), writes=("mus",))
        sc.add("v", lambda e: e.tensor_scalar(sqs[:], sqs[:], 1.0 / C, None, ALU.mult),
               reads=tuple(f"sqs{i}" for i in range(NTT)), writes=("sqs",))
        sc.add("v", lambda e: e.tensor_mul(rstd[:], mus[:], mus[:]),
               reads=("mus",), writes=("rstd",))
        sc.add("v", lambda e: e.tensor_tensor(rstd[:], sqs[:], rstd[:], ALU.subtract),
               reads=("sqs", "rstd"), writes=("rstd",))
        sc.add("v", lambda e: e.tensor_scalar(rstd[:], rstd[:], 1e-5, None, ALU.add),
               reads=("rstd",), writes=("rstd",))
        sc.add("s", lambda e: e.activation(rstd[:], rstd[:], AF.Sqrt),
               reads=("rstd",), writes=("rstd",))
        sc.add("v", lambda e: e.reciprocal(rstd[:], rstd[:]),
               reads=("rstd",), writes=("rstd",))

        want_h = hT_name is not None
        for i in range(NTT):
            j = i % 2
            sc.load(xa[j][:], src[i * 128:(i + 1) * 128, :], writes=(f"xa{j}",))
            sc.add("v", lambda e, j=j, i=i: e.tensor_scalar(
                xn_t[j][:], xa[j][:], mus[:, i:i + 1], rstd[:, i:i + 1],
                ALU.subtract, ALU.mult),
                reads=(f"xa{j}", "mus", "rstd"), writes=(f"xn{j}",))
            if xn_f32_name is not None:
                sc.store(d[xn_f32_name][i * 128:(i + 1) * 128, :], xn_t[j][:],
                         reads=(f"xn{j}",), writes=(f"dxn{i}",))
            sc.add("v", lambda e, j=j: e.tensor_copy(xnb[j][:], xn_t[j][:]),
                   reads=(f"xn{j}",), writes=(f"xnb{j}",))
            if want_h:
                sc.add("v", lambda e, j=j: e.tensor_mul(sq[j][:], xn_t[j][:], xn_t[j][:]),
                       reads=(f"xn{j}",), writes=(f"sq{j}",))
                sc.add("v", lambda e, j=j, i=i: e.reduce_sum(msum[:, i:i + 1], sq[j][:], axis=AX.X),
                       reads=(f"sq{j}",), writes=(f"ms{i}",))
            si = (i // 4) % 2
            for c in range(NCH):
                sc.add("t", lambda e, j=j, c=c: e.transpose(
                    tp[j][:, c * 128:(c + 1) * 128], xnb[j][:, c * 128:(c + 1) * 128], ident[:]),
                    reads=(f"xnb{j}", "ident"), writes=(f"tp{j}_{c}",))
            sc.add("v", lambda e, j=j, si=si, i=i: e.tensor_copy(
                slab[si][:, :, (i % 4) * 128:(i % 4 + 1) * 128],
                tp[j][:].rearrange("p (c f) -> p c f", c=NCH)),
                reads=tuple(f"tp{j}_{c}" for c in range(NCH)),
                writes=(f"slabw{si}_{i % 4}",), extra_deps=())
            if i % 4 == 3:
                i0 = i - 3
                sc.store(d[xnT_name][:, :, i0 * 128:(i0 + 4) * 128], slab[si][:],
                         reads=tuple(f"slabw{si}_{k}" for k in range(4)))
        if want_h:
            sc.add("v", lambda e: e.tensor_scalar(msum[:], msum[:], 1.0 / C, 1e-6, ALU.mult, ALU.add),
                   reads=tuple(f"ms{i}" for i in range(NTT)), writes=("msum",))
            sc.add("s", lambda e: e.activation(rstd2[:], msum[:], AF.Sqrt),
                   reads=("msum",), writes=("rstd2",))
            sc.add("v", lambda e: e.reciprocal(rstd2[:], rstd2[:]),
                   reads=("rstd2",), writes=("rstd2",))
            for i in range(NTT):
                j = i % 2
                ld = sc.load(xa[j][:], d[xn_f32_name][i * 128:(i + 1) * 128, :],
                             reads=(f"dxn{i}",), writes=(f"xa{j}",))
                sc.add("v", lambda e, j=j, i=i: e.tensor_scalar(
                    hb[j][:], xa[j][:], rstd2[:, i:i + 1], None, ALU.mult),
                    reads=(f"xa{j}", "rstd2"), writes=(f"hb{j}",))
                si = (i // 4) % 2
                for c in range(NCH):
                    sc.add("t", lambda e, j=j, c=c: e.transpose(
                        tp[j][:, c * 128:(c + 1) * 128], hb[j][:, c * 128:(c + 1) * 128], ident[:]),
                        reads=(f"hb{j}", "ident"), writes=(f"tp{j}_{c}",))
                sc.add("v", lambda e, j=j, si=si, i=i: e.tensor_copy(
                    slab2[si][:, :, (i % 4) * 128:(i % 4 + 1) * 128],
                    tp[j][:].rearrange("p (c f) -> p c f", c=NCH)),
                    reads=tuple(f"tp{j}_{c}" for c in range(NCH)),
                    writes=(f"slab2w{si}_{i % 4}",))
                if i % 4 == 3:
                    i0 = i - 3
                    sc.store(d[hT_name][:, :, i0 * 128:(i0 + 4) * 128], slab2[si][:],
                             reads=tuple(f"slab2w{si}_{k}" for k in range(4)))
        sc.flush()
    return


FEAT = "feat"
TOK = "tok"

DRAM_SPECS = {
    "xn_f32":  ([NTOK, C], F32),
    "xnT_bf":  ([128, NCH, NTOK], BF16),
    "hT_bf":   ([128, NCH, NTOK], BF16),
    "kT_f32":  ([128, NCH, NTOK], F32),
    "vT_f32":  ([128, NCH, NTOK], F32),
    "rT_f32":  ([128, NCH, NTOK], F32),
    "x1_f32":  ([NTOK, C], F32),
    "f0_f32":  ([NTOK, C], F32),
    "h2T_bf":  ([128, NCH, NTOK], BF16),
    "k2T_bf":  ([128, DH // 128, NTOK], BF16),
    "r2_f32":  ([NTOK, C], F32),
    "xb0a_f32": ([NTOK, C], F32),
    "out0_f32": ([NTOK, C], F32),
    "xn1T_bf": ([128, NCH, NTOK], BF16),
    "qT_bf":   ([128, NCH, NTOK], BF16),
    "kTq_bf":  ([128, NCH, NTOK], BF16),
    "v_bf":    ([NTOK, C], BF16),
    "P_bf":    ([T, T], BF16),
    "P2_bf":   ([T, T], BF16),
    "OT_bf":   ([128, NCH, NTOK], BF16),
    "att_f32": ([NTOK, C], F32),
    "hfb_bf":  ([128, DH // 128, NTOK], BF16),
    "g1_bf":   ([128, DH // 128, NTOK], BF16),
    "g2_bf":   ([128, DH // 128, NTOK], BF16),
}

WEIGHT_SPECS = {
    "W_key": ([C, C], F32), "W_value": ([C, C], F32), "W_recept": ([C, C], F32),
    "W_out": ([C, C], F32), "W_key2": ([C, DH], F32), "W_value2": ([DH, C], F32),
    "W_recept2": ([C, C], F32),
    "Ws1": ([C, ED], F32), "Ws2": ([ED, C], F32),
    "W1": ([NE, C, ED], F32), "W2": ([NE, ED, C], F32), "Wr": ([C, NE], F32),
    "wqT": ([C, C], F32), "wk_aT": ([C, LAT], F32), "wk_bT": ([LAT, C], F32),
    "wv_aT": ([C, LAT], F32), "wv_bT": ([LAT, C], F32), "woT": ([C, C], F32),
    "w1T": ([C, DH], F32), "w2T": ([C, DH], F32),
    "h1T": ([DH, DH], F32), "h2T": ([DH, DH], F32), "w3T": ([DH, C], F32),
    "w1s": ([DH], F32), "w2s": ([DH], F32), "h1s": ([DH], F32), "h2s": ([DH], F32),
    "w3s": ([C], F32),
    "ln1_w": ([C], F32), "ln2_w": ([C], F32),
    "time_decay": ([C], F32), "time_first": ([C], F32),
    "tmk": ([C], F32), "tmv": ([C], F32), "tmr": ([C], F32),
    "tmk2": ([C], F32), "tmr2": ([C], F32),
    "crep": ([T, 128], F32), "srep": ([T, 128], F32),
}


def declare_drams(nc, inspect=()):
    d = {}
    d["x"] = nc.dram_tensor("x", [NTOK, C], F32, kind="ExternalInput").ap()
    d["out"] = nc.dram_tensor("out", [NTOK, C], F32, kind="ExternalOutput").ap()
    for n, (sh, dt) in WEIGHT_SPECS.items():
        d[n] = nc.dram_tensor(n, sh, dt, kind="ExternalInput").ap()
    for n, (sh, dt) in DRAM_SPECS.items():
        kind = "ExternalOutput" if n in inspect else "Internal"
        d[n] = nc.dram_tensor(n, sh, dt, kind=kind).ap()
    return d


def rope_perm():
    p = []
    for h in range(NH):
        base = h * HD
        p += [base + 2 * j for j in range(8)]
        p += [base + 2 * j + 1 for j in range(8)]
        p += [base + j for j in range(DROPE, HD)]
    return np.array(p)


def host_prep(inp):
    P = rope_perm()
    asc = np.ascontiguousarray
    w = {
        "W_key": inp["L0_attn_W_key"], "W_value": inp["L0_attn_W_value"],
        "W_recept": inp["L0_attn_W_recept"], "W_out": inp["L0_attn_W_out"],
        "W_key2": inp["L0_attn_W_key2"], "W_value2": inp["L0_attn_W_value2"],
        "W_recept2": inp["L0_attn_W_recept2"],
        "Ws1": inp["L0_ffn_Ws1"], "Ws2": inp["L0_ffn_Ws2"],
        "W1": inp["L0_ffn_W1"], "W2": inp["L0_ffn_W2"], "Wr": inp["L0_ffn_Wr"],
        "wqT": asc(inp["L1_attn_wq"][P, :].T),
        "wk_aT": asc(inp["L1_attn_wk_a"].T),
        "wk_bT": asc(inp["L1_attn_wk_b"][P, :].T),
        "wv_aT": asc(inp["L1_attn_wv_a"].T),
        "wv_bT": asc(inp["L1_attn_wv_b"].T),
        "woT": asc(inp["L1_attn_wo"].T),
        "w1T": asc(inp["L1_ffn_w1_w"].T), "w2T": asc(inp["L1_ffn_w2_w"].T),
        "h1T": asc(inp["L1_ffn_h1_w"].T), "h2T": asc(inp["L1_ffn_h2_w"].T),
        "w3T": asc(inp["L1_ffn_w3_w"].T),
        "w1s": inp["L1_ffn_w1_s"], "w2s": inp["L1_ffn_w2_s"],
        "h1s": inp["L1_ffn_h1_s"], "h2s": inp["L1_ffn_h2_s"],
        "w3s": inp["L1_ffn_w3_s"],
        "ln1_w": inp["L0_attn_ln1_w"], "ln2_w": inp["L0_attn_ln2_w"],
        "time_decay": inp["L0_attn_time_decay"], "time_first": inp["L0_attn_time_first"],
        "tmk": inp["L0_attn_time_mix_k"], "tmv": inp["L0_attn_time_mix_v"],
        "tmr": inp["L0_attn_time_mix_r"],
        "tmk2": inp["L0_attn_time_mix_k2"], "tmr2": inp["L0_attn_time_mix_r2"],
    }
    inv = (1.0 / 10000.0 ** (np.arange(0, HD, 2, dtype=np.float32) / HD))[:8]
    ang = np.arange(NH, dtype=np.float32)[:, None] * inv[None, :]
    crow = np.cos(ang).reshape(-1).astype(np.float32)
    srow = np.sin(ang).reshape(-1).astype(np.float32)
    w["crep"] = asc(np.tile(crow[None, :], (T, 1)))
    w["srep"] = asc(np.tile(srow[None, :], (T, 1)))
    w = {k: np.ascontiguousarray(np.asarray(v, dtype=np.float32)) for k, v in w.items()}
    x = np.asarray(inp["L0_attn_x"], dtype=np.float32)
    shards = [np.ascontiguousarray(x[2 * c:2 * c + 2].reshape(NTOK, C)) for c in range(8)]
    return w, shards


def col_loads(sc, dst, vec_ap, rname):
    for c in range(dst.shape[1]):
        sc.load(dst[:, c:c + 1], vec_ap[c * 128:(c + 1) * 128].unsqueeze(1),
                writes=(f"{rname}{c}",))


def build_kvr(ctx, d):
    nc = ctx.nc
    nm = ctx.name("kvr")
    sc = Sched(ctx, nm)
    with ExitStack() as ses:
        sb = lambda n, s, dt: ses.enter_context(nc.sbuf_tensor(nm + n, s, dt))
        pb = lambda n, s, dt=F32: ses.enter_context(nc.psum_tensor(nm + n, s, dt))
        ln1 = sb("ln1", [128, NCH], F32)
        tm = {p: sb(p, [128, NCH], F32) for p in ("tmk", "tmv", "tmr")}
        om = {p: sb("o" + p, [128, NCH], F32) for p in ("tmk", "tmv", "tmr")}
        W = {p: sb("W" + p, [128, NCH, C], BF16) for p in ("k", "v", "r")}
        hT = sb("hT", [128, T], BF16)
        tmp = sb("tmp", [128, T], F32)
        mix = {p: sb("mx" + p, [128, NCH, T], BF16) for p in ("k", "v", "r")}
        ps = [pb(f"ps{j}", [128, 512]) for j in range(2)]
        ev = [sb(f"ev{j}", [128, 512], F32) for j in range(2)]

        col_loads(sc, ln1, d["ln1_w"], "ln1c")
        for p, src in (("tmk", "tmk"), ("tmv", "tmv"), ("tmr", "tmr")):
            col_loads(sc, tm[p], d[p], p + "c")
            sc.add("v", lambda e, p=p: e.tensor_scalar(om[p][:], tm[p][:], -1.0, 1.0, ALU.mult, ALU.add),
                   reads=tuple(f"{p}c{c}" for c in range(NCH)), writes=("o" + p,))
        for p, wn in (("k", "W_key"), ("v", "W_value"), ("r", "W_recept")):
            for ci in range(NCH):
                sc.load(W[p][:, ci, :], d[wn][ci * 128:(ci + 1) * 128, :],
                        writes=(f"W{p}{ci}",))
                sc.add("v", lambda e, p=p, ci=ci: e.tensor_scalar(
                    W[p][:, ci, :], W[p][:, ci, :], ln1[:, ci:ci + 1], None, ALU.mult),
                    reads=(f"W{p}{ci}",) + tuple(f"ln1c{c}" for c in range(NCH)),
                    writes=(f"W{p}{ci}",))

        for b in range(B):
            sc.fence_all()
            for c in range(NCH):
                sc.load(hT[:], d["hT_bf"][:, c, b * T:(b + 1) * T], writes=("hT",))
                for p, pr in (("k", "tmk"), ("v", "tmv"), ("r", "tmr")):
                    sc.add("v", lambda e, p=p, pr=pr, c=c: e.tensor_scalar(
                        tmp[:], hT[:], tm[pr][:, c:c + 1], None, ALU.mult),
                        reads=("hT",), writes=("tmp",))
                    sc.add("v", lambda e, p=p, pr=pr, c=c: e.scalar_tensor_tensor(
                        mix[p][:, c, 1:T], hT[:, 0:T - 1], om[pr][:, c:c + 1],
                        tmp[:, 1:T], ALU.mult, ALU.add),
                        reads=("hT", "tmp", "o" + pr), writes=(f"mix{p}{c}",))
                    sc.add("v", lambda e, p=p, c=c: e.tensor_copy(
                        mix[p][:, c, 0:1], tmp[:, 0:1]),
                        reads=("tmp",), writes=(f"mix{p}{c}",))
            jj = 0
            for p, outn, sig in (("k", "kT_f32", False), ("v", "vT_f32", False),
                                 ("r", "rT_f32", True)):
                for co in range(NCH):
                    for n in range(T // 512):
                        j = jj % 2
                        jj += 1

                        def mmf(e, p=p, co=co, n=n, j=j):
                            for ci in range(NCH):
                                mm = e.matmul(ps[j][:], W[p][:, ci, co * 128:(co + 1) * 128],
                                              mix[p][:, ci, n * 512:(n + 1) * 512],
                                              start=(ci == 0), stop=(ci == NCH - 1))
                            return mm
                        sc.add("t", mmf,
                               reads=tuple(f"W{p}{ci}" for ci in range(NCH)) +
                                     tuple(f"mix{p}{ci}" for ci in range(NCH)),
                               writes=(f"ps{j}",))
                        if sig:
                            sc.add("s", lambda e, j=j: e.activation(ev[j][:], ps[j][:], AF.Sigmoid),
                                   reads=(f"ps{j}",), writes=(f"ev{j}",))
                        else:
                            sc.add("v", lambda e, j=j: e.tensor_copy(ev[j][:], ps[j][:]),
                                   reads=(f"ps{j}",), writes=(f"ev{j}",))
                        sc.store(d[outn][:, co, b * T + n * 512: b * T + (n + 1) * 512],
                                 ev[j][:], reads=(f"ev{j}",))
        sc.flush()


def build_wkv(ctx, d):
    nc = ctx.nc
    nm = ctx.name("wkv")
    sc = Sched(ctx, nm)
    with ExitStack() as ses:
        sb = lambda n, s, dt: ses.enter_context(nc.sbuf_tensor(nm + n, s, dt))
        pb = lambda n, s, dt=F32: ses.enter_context(nc.psum_tensor(nm + n, s, dt))
        rw = sb("rw", [128, NCH, NTOK], BF16)
        Wo = sb("Wo", [128, NCH, C], BF16)
        ew = sb("ew", [128, NCH], F32)
        uu = sb("uu", [128, NCH], F32)
        for ci in range(NCH):
            sc.load(Wo[:, ci, :], d["W_out"][ci * 128:(ci + 1) * 128, :],
                    writes=(f"Wo{ci}",))
        col_loads(sc, ew, d["time_decay"], "td")
        col_loads(sc, uu, d["time_first"], "uc")
        sc.add("s", lambda e: e.activation(ew[:], ew[:], AF.Exp),
               reads=tuple(f"td{c}" for c in range(NCH)), writes=("ew",))
        sc.add("v", lambda e: e.tensor_scalar(ew[:], ew[:], -1.0, None, ALU.mult),
               reads=("ew",), writes=("ew",))
        sc.add("v", lambda e: e.tensor_scalar(ew[:], ew[:], 20.0, -20.0, ALU.min, ALU.max),
               reads=("ew",), writes=("ew",))
        sc.add("s", lambda e: e.activation(ew[:], ew[:], AF.Exp),
               reads=("ew",), writes=("ew",))

        with ExitStack() as scan_scope:
            sb2 = lambda n, s, dt: scan_scope.enter_context(nc.sbuf_tensor(nm + n, s, dt))
            kt = [sb2(f"kt{j}", [128, T], F32) for j in range(2)]
            vt = [sb2(f"vt{j}", [128, T], F32) for j in range(2)]
            rt = [sb2(f"rt{j}", [128, T], F32) for j in range(2)]
            e1 = sb2("e1", [128, T], F32)
            ekt = sb2("ekt", [128, T], F32)
            ekv = sb2("ekv", [128, T], F32)
            ektv = sb2("ektv", [128, T], F32)
            At = sb2("At", [128, T], F32)
            Bt = sb2("Bt", [128, T], F32)
            den = sb2("den", [128, T], F32)

            for it in range(NCH * B):
                c, b = it // B, it % B
                j = it % 2
                sc.load(kt[j][:], d["kT_f32"][:, c, b * T:(b + 1) * T], writes=(f"kt{j}",))
                sc.load(vt[j][:], d["vT_f32"][:, c, b * T:(b + 1) * T], writes=(f"vt{j}",))
                sc.load(rt[j][:], d["rT_f32"][:, c, b * T:(b + 1) * T], writes=(f"rt{j}",))
                A = lambda fn, reads, writes: sc.add("v", fn, reads=reads, writes=writes)
                S = lambda fn, reads, writes: sc.add("s", fn, reads=reads, writes=writes)
                A(lambda e, j=j: e.tensor_scalar(e1[:], kt[j][:], 20.0, -20.0, ALU.min, ALU.max),
                  (f"kt{j}",), ("e1",))
                S(lambda e: e.activation(e1[:], e1[:], AF.Exp), ("e1",), ("e1",))
                A(lambda e, j=j, c=c: e.tensor_scalar(ekt[:], kt[j][:], uu[:, c:c + 1], 20.0, ALU.add, ALU.min),
                  (f"kt{j}", "uu"), ("ekt",))
                A(lambda e: e.tensor_scalar(ekt[:], ekt[:], -20.0, None, ALU.max),
                  ("ekt",), ("ekt",))
                S(lambda e: e.activation(ekt[:], ekt[:], AF.Exp), ("ekt",), ("ekt",))
                A(lambda e, j=j: e.tensor_mul(ekv[:], e1[:], vt[j][:]), ("e1", f"vt{j}"), ("ekv",))
                A(lambda e, j=j: e.tensor_mul(ektv[:], ekt[:], vt[j][:]), ("ekt", f"vt{j}"), ("ektv",))
                A(lambda e, c=c: e.tensor_tensor_scan(At[:], ew[:, c:c + 1].broadcast_to([128, T]),
                                                      ekv[:], 0.0, ALU.mult, ALU.add),
                  ("ekv", "ew"), ("At",))
                A(lambda e, c=c: e.tensor_tensor_scan(Bt[:], ew[:, c:c + 1].broadcast_to([128, T]),
                                                      e1[:], 0.0, ALU.mult, ALU.add),
                  ("e1", "ew"), ("Bt",))
                A(lambda e: e.tensor_tensor(ektv[:, 1:T], At[:, 0:T - 1], ektv[:, 1:T], ALU.add),
                  ("At", "ektv"), ("ektv",))
                A(lambda e: e.tensor_scalar(den[:, 0:1], ekt[:, 0:1], 1e-8, None, ALU.add),
                  ("ekt",), ("den",))
                A(lambda e: e.tensor_tensor(den[:, 1:T], Bt[:, 0:T - 1], ekt[:, 1:T], ALU.add),
                  ("Bt", "ekt"), ("den",))
                A(lambda e: e.tensor_scalar(den[:, 1:T], den[:, 1:T], 1e-8, None, ALU.add),
                  ("den",), ("den",))
                A(lambda e: e.reciprocal(den[:], den[:]), ("den",), ("den",))
                A(lambda e: e.tensor_mul(ektv[:], ektv[:], den[:]), ("ektv", "den"), ("ektv",))
                A(lambda e, j=j, c=c, b=b: e.tensor_mul(rw[:, c, b * T:(b + 1) * T],
                                                        ektv[:], rt[j][:]),
                  ("ektv", f"rt{j}"), (f"rw{c}_{b}",))
            sc.fence_all()

        with ExitStack() as att_scope:
            sb3 = lambda n, s, dt: att_scope.enter_context(nc.sbuf_tensor(nm + n, s, dt))
            xnt = [sb3(f"xnt{j}", [128, C], F32) for j in range(2)]
            ot = [sb3(f"ot{j}", [128, C], F32) for j in range(2)]
            po = [pb(f"po{j}", [128, 1024]) for j in range(2)]
            for i in range(NTT):
                j = i % 2
                sc.load(xnt[j][:], d["xn_f32"][i * 128:(i + 1) * 128, :], writes=(f"xnt{j}",))

                def mmf(e, i=i, j=j):
                    for ci in range(NCH):
                        for h in range(2):
                            mm = e.matmul(po[j][:, h * 512:(h + 1) * 512],
                                          rw[:, ci, i * 128:(i + 1) * 128],
                                          Wo[:, ci, h * 512:(h + 1) * 512],
                                          start=(ci == 0), stop=(ci == NCH - 1))
                    return mm
                rwres = tuple(f"rw{ci}_{bb}" for ci in range(NCH) for bb in range(B))
                sc.add("t", mmf, reads=rwres + tuple(f"Wo{ci}" for ci in range(NCH)),
                       writes=(f"po{j}",))
                sc.add("v", lambda e, j=j: e.tensor_tensor(ot[j][:], po[j][:], xnt[j][:], ALU.add),
                       reads=(f"po{j}", f"xnt{j}"), writes=(f"ot{j}",))
                sc.store(d["x1_f32"][i * 128:(i + 1) * 128, :], ot[j][:], reads=(f"ot{j}",))
        sc.flush()


def build_moe(ctx, d):
    nc = ctx.nc
    nm = ctx.name("moe")
    sc = Sched(ctx, nm)
    NT5 = 512
    with ExitStack() as ses:
        sb = lambda n, s, dt: ses.enter_context(nc.sbuf_tensor(nm + n, s, dt))
        pb = lambda n, s, dt=F32: ses.enter_context(nc.psum_tensor(nm + n, s, dt))
        Wr = sb("Wr", [128, NCH, NE], BF16)
        Ws1 = sb("Ws1", [128, NCH, ED], BF16)
        Ws2 = sb("Ws2", [128, ED // 128, C], BF16)
        W1 = sb("W1", [128, NCH, NE * ED], BF16)
        W2 = sb("W2", [128, ED // 128, NE * C], BF16)
        xnt = [sb(f"xnt{j}", [128, NCH, NT5], BF16) for j in range(2)]
        s1 = sb("s1", [128, ED // 128, NT5], BF16)
        he = sb("he", [128, ED // 128, NT5], BF16)
        relu_t = sb("relu", [128, NT5], F32)
        lg = sb("lg", [128, NE], F32)
        ra = sb("ra", [128, 8], F32)
        prob = sb("prob", [128, NTT, NE], F32)
        f0 = [sb(f"f0_{t}", [128, C], F32) for t in range(4)]
        pss = [pb(f"pss{j}", [128, NT5]) for j in range(2)]
        psy = [pb(f"psy{j}", [128, C]) for j in range(2)]
        pslg = pb("pslg", [128, NE])

        for ci in range(NCH):
            sc.load(Wr[:, ci, :], d["Wr"][ci * 128:(ci + 1) * 128, :], writes=("Wr",))
            sc.load(Ws1[:, ci, :], d["Ws1"][ci * 128:(ci + 1) * 128, :], writes=("Ws1",))
            for e in range(NE):
                sc.load(W1[:, ci, e * ED:(e + 1) * ED], d["W1"][e, ci * 128:(ci + 1) * 128, :],
                        writes=("W1",))
        for edc in range(ED // 128):
            sc.load(Ws2[:, edc, :], d["Ws2"][edc * 128:(edc + 1) * 128, :], writes=("Ws2",))
            for e in range(NE):
                sc.load(W2[:, edc, e * C:(e + 1) * C], d["W2"][e, edc * 128:(edc + 1) * 128, :],
                        writes=("W2",))

        for i in range(NTT):
            j = i % 2
            sc.load(xnt[j][:, :, 0:128], d["xnT_bf"][:, :, i * 128:(i + 1) * 128],
                    writes=(f"xq{j}",))

            def mmlg(e, j=j):
                for ci in range(NCH):
                    mm = e.matmul(pslg[:], xnt[j][:, ci, 0:128], Wr[:, ci, :],
                                  start=(ci == 0), stop=(ci == NCH - 1))
                return mm
            sc.add("t", mmlg, reads=(f"xq{j}", "Wr"), writes=("pslg",))
            sc.add("v", lambda e: e.tensor_copy(lg[:], pslg[:]), reads=("pslg",), writes=("lg",))
            sc.add("v", lambda e: e.reduce_max(ra[:, 0:1], lg[:], axis=AX.X), reads=("lg",), writes=("ra",))
            sc.add("v", lambda e, i=i: e.tensor_scalar(prob[:, i, :], lg[:], ra[:, 0:1], None, ALU.is_equal),
                   reads=("lg", "ra"), writes=(f"pr{i}",))
            sc.add("v", lambda e, i=i: e.reduce_sum(ra[:, 1:2], prob[:, i, :], axis=AX.X),
                   reads=(f"pr{i}",), writes=("ra",))
            sc.add("v", lambda e, i=i: e.scalar_tensor_tensor(lg[:], prob[:, i, :], -1e30, lg[:], ALU.mult, ALU.add),
                   reads=(f"pr{i}", "lg"), writes=("lg",))
            sc.add("v", lambda e: e.reduce_max(ra[:, 2:3], lg[:], axis=AX.X), reads=("lg",), writes=("ra",))
            sc.add("v", lambda e: e.tensor_tensor(ra[:, 3:4], ra[:, 2:3], ra[:, 0:1], ALU.subtract),
                   reads=("ra",), writes=("ra",))
            sc.add("s", lambda e: e.activation(ra[:, 4:5], ra[:, 3:4], AF.Exp), reads=("ra",), writes=("ra",))
            sc.add("v", lambda e: e.tensor_scalar(ra[:, 4:5], ra[:, 4:5], 1.0, None, ALU.add),
                   reads=("ra",), writes=("ra",))
            sc.add("v", lambda e: e.reciprocal(ra[:, 4:5], ra[:, 4:5]), reads=("ra",), writes=("ra",))
            sc.add("v", lambda e: e.tensor_scalar(ra[:, 5:6], ra[:, 4:5], 1.0, None, ALU.is_ge),
                   reads=("ra",), writes=("ra",))
            sc.add("v", lambda e: e.tensor_scalar(ra[:, 6:7], ra[:, 1:2], 1.0, None, ALU.is_le),
                   reads=("ra",), writes=("ra",))
            sc.add("v", lambda e: e.tensor_mul(ra[:, 5:6], ra[:, 5:6], ra[:, 6:7]),
                   reads=("ra",), writes=("ra",))
            sc.add("v", lambda e, i=i: e.tensor_scalar(prob[:, i, :], prob[:, i, :], ra[:, 5:6], None, ALU.mult),
                   reads=(f"pr{i}", "ra"), writes=(f"pr{i}",))

        for n in range(NTOK // NT5):
            j = n % 2
            sc.load(xnt[j][:], d["xnT_bf"][:, :, n * NT5:(n + 1) * NT5], writes=(f"xq{j}",))
            for edc in range(ED // 128):
                jp = edc % 2

                def mms1(e, edc=edc, j=j, jp=jp):
                    for ci in range(NCH):
                        mm = e.matmul(pss[jp][:], Ws1[:, ci, edc * 128:(edc + 1) * 128],
                                      xnt[j][:, ci, :], start=(ci == 0), stop=(ci == NCH - 1))
                    return mm
                sc.add("t", mms1, reads=(f"xq{j}", "Ws1"), writes=(f"pss{jp}",))
                sc.add("s", lambda e, jp=jp: e.activation(relu_t[:], pss[jp][:], AF.Relu),
                       reads=(f"pss{jp}",), writes=("relu",))
                sc.add("v", lambda e, edc=edc: e.tensor_mul(s1[:, edc, :], relu_t[:], relu_t[:]),
                       reads=("relu",), writes=(f"s1_{edc}",))
            for t in range(4):
                jp = t % 2

                def mmsh(e, t=t, jp=jp):
                    for edc in range(ED // 128):
                        for h in range(2):
                            mm = e.matmul(psy[jp][:, h * 512:(h + 1) * 512],
                                          s1[:, edc, t * 128:(t + 1) * 128],
                                          Ws2[:, edc, h * 512:(h + 1) * 512],
                                          start=(edc == 0), stop=(edc == ED // 128 - 1))
                    return mm
                sc.add("t", mmsh, reads=tuple(f"s1_{e2}" for e2 in range(4)) + ("Ws2",),
                       writes=(f"psy{jp}",))
                sc.add("v", lambda e, t=t, jp=jp: e.tensor_copy(f0[t][:], psy[jp][:]),
                       reads=(f"psy{jp}",), writes=(f"f0_{t}",))
            for ex in range(NE):
                for edc in range(ED // 128):
                    jp = edc % 2

                    def mmhe(e, ex=ex, edc=edc, j=j, jp=jp):
                        for ci in range(NCH):
                            mm = e.matmul(pss[jp][:], W1[:, ci, ex * ED + edc * 128: ex * ED + (edc + 1) * 128],
                                          xnt[j][:, ci, :], start=(ci == 0), stop=(ci == NCH - 1))
                        return mm
                    sc.add("t", mmhe, reads=(f"xq{j}", "W1"), writes=(f"pss{jp}",))
                    sc.add("s", lambda e, jp=jp: e.activation(relu_t[:], pss[jp][:], AF.Relu),
                           reads=(f"pss{jp}",), writes=("relu",))
                    sc.add("v", lambda e, edc=edc: e.tensor_mul(he[:, edc, :], relu_t[:], relu_t[:]),
                           reads=("relu",), writes=(f"he_{edc}",))
                for t in range(4):
                    jp = t % 2
                    ti = n * 4 + t

                    def mmy(e, ex=ex, t=t, jp=jp):
                        for edc in range(ED // 128):
                            for h in range(2):
                                mm = e.matmul(psy[jp][:, h * 512:(h + 1) * 512],
                                              he[:, edc, t * 128:(t + 1) * 128],
                                              W2[:, edc, ex * C + h * 512: ex * C + (h + 1) * 512],
                                              start=(edc == 0), stop=(edc == ED // 128 - 1))
                        return mm
                    sc.add("t", mmy, reads=tuple(f"he_{e2}" for e2 in range(4)) + ("W2",),
                           writes=(f"psy{jp}",))
                    sc.add("v", lambda e, t=t, jp=jp, ti=ti, ex=ex: e.scalar_tensor_tensor(
                        f0[t][:], psy[jp][:], prob[:, ti, ex:ex + 1], f0[t][:], ALU.mult, ALU.add),
                        reads=(f"psy{jp}", f"pr{ti}", f"f0_{t}"), writes=(f"f0_{t}",))
            for t in range(4):
                ti = n * 4 + t
                sc.store(d["f0_f32"][ti * 128:(ti + 1) * 128, :], f0[t][:], reads=(f"f0_{t}",))
        sc.flush()


def build_cmix(ctx, d):
    nc = ctx.nc
    nm = ctx.name("cmx")
    sc = Sched(ctx, nm)
    NHC = DH // 128

    with ExitStack() as ses:
        sb = lambda n, s, dt: ses.enter_context(nc.sbuf_tensor(nm + n, s, dt))
        pb = lambda n, s, dt=BF16: ses.enter_context(nc.psum_tensor(nm + n, s, dt))
        xa = [sb(f"xa{j}", [128, C], F32) for j in range(2)]
        sq = [sb(f"sq{j}", [128, C], F32) for j in range(2)]
        hb = [sb(f"hb{j}", [128, C], BF16) for j in range(2)]
        msum = sb("msum", [128, NTT], F32)
        rstd2 = sb("rstd2", [128, NTT], F32)
        ident = sb("ident", [128, 128], BF16)
        tp = [pb(f"tp{j}", [128, 1024]) for j in range(2)]
        slab = [sb(f"sl{j}", [128, NCH, 512], BF16) for j in range(2)]
        make_identity_bf(nc, sc, ident)
        for i in range(NTT):
            j = i % 2
            sc.load(xa[j][:], d["x1_f32"][i * 128:(i + 1) * 128, :], writes=(f"xa{j}",))
            sc.add("v", lambda e, j=j: e.tensor_mul(sq[j][:], xa[j][:], xa[j][:]),
                   reads=(f"xa{j}",), writes=(f"sq{j}",))
            sc.add("v", lambda e, j=j, i=i: e.reduce_sum(msum[:, i:i + 1], sq[j][:], axis=AX.X),
                   reads=(f"sq{j}",), writes=(f"ms{i}",))
        sc.add("v", lambda e: e.tensor_scalar(msum[:], msum[:], 1.0 / C, 1e-6, ALU.mult, ALU.add),
               reads=tuple(f"ms{i}" for i in range(NTT)), writes=("msum",))
        sc.add("s", lambda e: e.activation(rstd2[:], msum[:], AF.Sqrt),
               reads=("msum",), writes=("rstd2",))
        sc.add("v", lambda e: e.reciprocal(rstd2[:], rstd2[:]), reads=("rstd2",), writes=("rstd2",))
        for i in range(NTT):
            j = i % 2
            sc.load(xa[j][:], d["x1_f32"][i * 128:(i + 1) * 128, :], writes=(f"xa{j}",))
            sc.add("v", lambda e, j=j, i=i: e.tensor_scalar(
                hb[j][:], xa[j][:], rstd2[:, i:i + 1], None, ALU.mult),
                reads=(f"xa{j}", "rstd2"), writes=(f"hb{j}",))
            si = (i // 4) % 2
            for c in range(NCH):
                sc.add("t", lambda e, j=j, c=c: e.transpose(
                    tp[j][:, c * 128:(c + 1) * 128], hb[j][:, c * 128:(c + 1) * 128], ident[:]),
                    reads=(f"hb{j}", "ident"), writes=(f"tp{j}_{c}",))
            sc.add("v", lambda e, j=j, si=si, i=i: e.tensor_copy(
                slab[si][:, :, (i % 4) * 128:(i % 4 + 1) * 128],
                tp[j][:].rearrange("p (c f) -> p c f", c=NCH)),
                reads=tuple(f"tp{j}_{c}" for c in range(NCH)), writes=(f"slabw{si}_{i % 4}",))
            if i % 4 == 3:
                i0 = i - 3
                sc.store(d["h2T_bf"][:, :, i0 * 128:(i0 + 4) * 128], slab[si][:],
                         reads=tuple(f"slabw{si}_{k}" for k in range(4)),
                         writes=(f"dh2T{i0 // 4}",))
        sc.fence_all()

    with ExitStack() as ses:
        sb = lambda n, s, dt: ses.enter_context(nc.sbuf_tensor(nm + "b" + n, s, dt))
        pb = lambda n, s, dt=F32: ses.enter_context(nc.psum_tensor(nm + "b" + n, s, dt))
        ln2 = sb("ln2", [128, NCH], F32)
        tmk2 = sb("tmk2", [128, NCH], F32)
        tmr2 = sb("tmr2", [128, NCH], F32)
        omk2 = sb("omk2", [128, NCH], F32)
        omr2 = sb("omr2", [128, NCH], F32)
        Wk2 = sb("Wk2", [128, NCH, DH], BF16)
        Wr2 = sb("Wr2", [128, NCH, C], BF16)
        hT = sb("hT", [128, T], BF16)
        tmp = sb("tmp", [128, T], F32)
        mk2 = sb("mk2", [128, NCH, T], BF16)
        mr2 = sb("mr2", [128, NCH, T], BF16)
        ps = [pb(f"ps{j}", [128, 512]) for j in range(2)]
        ev = [sb(f"ev{j}", [128, 512], BF16) for j in range(2)]
        relu_t = sb("relu", [128, 512], F32)
        psr = [pb(f"psr{j}", [128, C]) for j in range(2)]
        evr = [sb(f"evr{j}", [128, C], F32) for j in range(2)]
        col_loads(sc, ln2, d["ln2_w"], "ln2c")
        col_loads(sc, tmk2, d["tmk2"], "tmk2c")
        col_loads(sc, tmr2, d["tmr2"], "tmr2c")
        sc.add("v", lambda e: e.tensor_scalar(omk2[:], tmk2[:], -1.0, 1.0, ALU.mult, ALU.add),
               reads=tuple(f"tmk2c{c}" for c in range(NCH)), writes=("omk2",))
        sc.add("v", lambda e: e.tensor_scalar(omr2[:], tmr2[:], -1.0, 1.0, ALU.mult, ALU.add),
               reads=tuple(f"tmr2c{c}" for c in range(NCH)), writes=("omr2",))
        for ci in range(NCH):
            sc.load(Wk2[:, ci, :], d["W_key2"][ci * 128:(ci + 1) * 128, :], writes=(f"Wk2{ci}",))
            sc.add("v", lambda e, ci=ci: e.tensor_scalar(
                Wk2[:, ci, :], Wk2[:, ci, :], ln2[:, ci:ci + 1], None, ALU.mult),
                reads=(f"Wk2{ci}",) + tuple(f"ln2c{c}" for c in range(NCH)), writes=(f"Wk2{ci}",))
            sc.load(Wr2[:, ci, :], d["W_recept2"][ci * 128:(ci + 1) * 128, :], writes=(f"Wr2{ci}",))
            sc.add("v", lambda e, ci=ci: e.tensor_scalar(
                Wr2[:, ci, :], Wr2[:, ci, :], ln2[:, ci:ci + 1], None, ALU.mult),
                reads=(f"Wr2{ci}",) + tuple(f"ln2c{c}" for c in range(NCH)), writes=(f"Wr2{ci}",))
        jj = 0
        for b in range(B):
            for c in range(NCH):
                sc.load(hT[:], d["h2T_bf"][:, c, b * T:(b + 1) * T],
                        reads=(f"dh2T{(b * NCH + c) // ((NCH * B) // 8) if False else 0}",) if False else
                              tuple(f"dh2T{kk}" for kk in range(8)),
                        writes=("hT",))
                for mx, tmc, omc in ((mk2, tmk2, omk2), (mr2, tmr2, omr2)):
                    mxn = "mk2" if mx is mk2 else "mr2"
                    sc.add("v", lambda e, c=c, tmc=tmc: e.tensor_scalar(
                        tmp[:], hT[:], tmc[:, c:c + 1], None, ALU.mult),
                        reads=("hT",), writes=("tmp",))
                    omn = "omk2" if mx is mk2 else "omr2"
                    sc.add("v", lambda e, c=c, omc=omc, mx=mx: e.scalar_tensor_tensor(
                        mx[:, c, 1:T], hT[:, 0:T - 1], omc[:, c:c + 1], tmp[:, 1:T], ALU.mult, ALU.add),
                        reads=("hT", "tmp", omn), writes=(f"{mxn}{c}",))
                    sc.add("v", lambda e, c=c, mx=mx: e.tensor_copy(mx[:, c, 0:1], tmp[:, 0:1]),
                           reads=("tmp",), writes=(f"{mxn}{c}",))
            for co in range(NHC):
                for n in range(T // 512):
                    j = jj % 2
                    jj += 1

                    def mmf(e, co=co, n=n, j=j):
                        for ci in range(NCH):
                            mm = e.matmul(ps[j][:], Wk2[:, ci, co * 128:(co + 1) * 128],
                                          mk2[:, ci, n * 512:(n + 1) * 512],
                                          start=(ci == 0), stop=(ci == NCH - 1))
                        return mm
                    sc.add("t", mmf, reads=tuple(f"Wk2{ci}" for ci in range(NCH)) +
                           tuple(f"mk2{ci}" for ci in range(NCH)), writes=(f"ps{j}",))
                    sc.add("s", lambda e, j=j: e.activation(relu_t[:], ps[j][:], AF.Relu),
                           reads=(f"ps{j}",), writes=("relu",))
                    sc.add("v", lambda e, j=j: e.tensor_mul(ev[j][:], relu_t[:], relu_t[:]),
                           reads=("relu",), writes=(f"ev{j}",))
                    sc.store(d["k2T_bf"][:, co, b * T + n * 512: b * T + (n + 1) * 512], ev[j][:],
                             reads=(f"ev{j}",), writes=(f"dk2_{b}_{co}_{n}",))
            for ti in range(T // 128):
                j = ti % 2
                i = b * (T // 128) + ti

                def mmr(e, ti=ti, j=j):
                    for ci in range(NCH):
                        for h in range(2):
                            mm = e.matmul(psr[j][:, h * 512:(h + 1) * 512],
                                          mr2[:, ci, ti * 128:(ti + 1) * 128],
                                          Wr2[:, ci, h * 512:(h + 1) * 512],
                                          start=(ci == 0), stop=(ci == NCH - 1))
                    return mm
                sc.add("t", mmr, reads=tuple(f"Wr2{ci}" for ci in range(NCH)) +
                       tuple(f"mr2{ci}" for ci in range(NCH)), writes=(f"psr{j}",))
                sc.add("s", lambda e, j=j: e.activation(evr[j][:], psr[j][:], AF.Sigmoid),
                       reads=(f"psr{j}",), writes=(f"evr{j}",))
                sc.store(d["r2_f32"][i * 128:(i + 1) * 128, :], evr[j][:],
                         reads=(f"evr{j}",), writes=(f"dr2_{i}",))
        sc.fence_all()

    with ExitStack() as ses:
        sb = lambda n, s, dt: ses.enter_context(nc.sbuf_tensor(nm + "v" + n, s, dt))
        pb = lambda n, s, dt=F32: ses.enter_context(nc.psum_tensor(nm + "v" + n, s, dt))
        Wv2 = sb("Wv2", [128, NHC, C], BF16)
        k2g = [sb(f"k2g{j}", [128, NHC, 512], BF16) for j in range(2)]
        r2t = [sb(f"r2t{j}", [128, C], F32) for j in range(2)]
        x1t = [sb(f"x1t{j}", [128, C], F32) for j in range(2)]
        ot = [sb(f"ot{j}", [128, C], F32) for j in range(2)]
        psv = [pb(f"psv{j}", [128, C]) for j in range(2)]
        for hc in range(NHC):
            sc.load(Wv2[:, hc, :], d["W_value2"][hc * 128:(hc + 1) * 128, :], writes=("Wv2",))
        for g in range(NTOK // 512):
            jg = g % 2
            b, n4 = g // 4, g % 4
            for cos_ in range(0, NHC, 8):
                sc.load(k2g[jg][:, cos_:cos_ + 8, :],
                        d["k2T_bf"][:, cos_:cos_ + 8, g * 512:(g + 1) * 512],
                        reads=tuple(f"dk2_{b}_{co}_{n4}" for co in range(cos_, cos_ + 8)),
                        writes=(f"k2g{jg}",))
            for t4 in range(4):
                i = g * 4 + t4
                j = i % 2
                sc.load(r2t[j][:], d["r2_f32"][i * 128:(i + 1) * 128, :],
                        reads=(f"dr2_{i}",), writes=(f"r2t{j}",))
                sc.load(x1t[j][:], d["x1_f32"][i * 128:(i + 1) * 128, :], writes=(f"x1t{j}",))

                def mmv(e, t4=t4, j=j, jg=jg):
                    for hc in range(NHC):
                        for h in range(2):
                            mm = e.matmul(psv[j][:, h * 512:(h + 1) * 512],
                                          k2g[jg][:, hc, t4 * 128:(t4 + 1) * 128],
                                          Wv2[:, hc, h * 512:(h + 1) * 512],
                                          start=(hc == 0), stop=(hc == NHC - 1))
                    return mm
                sc.add("t", mmv, reads=(f"k2g{jg}", "Wv2"), writes=(f"psv{j}",))
                sc.add("v", lambda e, j=j: e.tensor_tensor(ot[j][:], r2t[j][:], psv[j][:], ALU.mult),
                       reads=(f"r2t{j}", f"psv{j}"), writes=(f"ot{j}",))
                sc.add("v", lambda e, j=j: e.tensor_tensor(ot[j][:], ot[j][:], x1t[j][:], ALU.add),
                       reads=(f"ot{j}", f"x1t{j}"), writes=(f"ot{j}",))
                sc.store(d["xb0a_f32"][i * 128:(i + 1) * 128, :], ot[j][:], reads=(f"ot{j}",))
        sc.flush()


def build_close0_ln1(ctx, d):
    nc = ctx.nc
    nm = ctx.name("cl0")
    sc = Sched(ctx, nm)
    with ExitStack() as ses:
        sb = lambda n, s, dt: ses.enter_context(nc.sbuf_tensor(nm + n, s, dt))
        xt = [sb(f"xt{j}", [128, C], F32) for j in range(2)]
        at = [sb(f"at{j}", [128, C], F32) for j in range(2)]
        ft = [sb(f"ft{j}", [128, C], F32) for j in range(2)]
        for i in range(NTT):
            j = i % 2
            sc.load(xt[j][:], d["x"][i * 128:(i + 1) * 128, :], writes=(f"xt{j}",))
            sc.load(at[j][:], d["xb0a_f32"][i * 128:(i + 1) * 128, :], writes=(f"at{j}",))
            sc.load(ft[j][:], d["f0_f32"][i * 128:(i + 1) * 128, :], writes=(f"ft{j}",))
            sc.add("v", lambda e, j=j: e.tensor_tensor(xt[j][:], xt[j][:], at[j][:], ALU.add),
                   reads=(f"xt{j}", f"at{j}"), writes=(f"xt{j}",))
            sc.add("v", lambda e, j=j: e.tensor_tensor(xt[j][:], xt[j][:], ft[j][:], ALU.add),
                   reads=(f"xt{j}", f"ft{j}"), writes=(f"xt{j}",))
            sc.store(d["out0_f32"][i * 128:(i + 1) * 128, :], xt[j][:], reads=(f"xt{j}",))
        sc.flush()
    build_ln_section(ctx, d, "out0_f32", None, "xn1T_bf", None, "ln1")


def weave(sc, lists, weights):
    idx = [0] * len(lists)
    while any(idx[i] < len(lists[i]) for i in range(len(lists))):
        for li, wt in enumerate(weights):
            for _ in range(wt):
                if idx[li] < len(lists[li]):
                    lists[li][idx[li]]()
                    idx[li] += 1


def build_L1a(ctx, d):
    nc = ctx.nc
    nm = ctx.name("l1a")
    sc = Sched(ctx, nm)
    NLC = LAT // 128
    with ExitStack() as ses:
        sb = lambda n, s, dt: ses.enter_context(nc.sbuf_tensor(nm + n, s, dt))
        pb = lambda n, s, dt=F32: ses.enter_context(nc.psum_tensor(nm + n, s, dt))
        xf = sb("xf", [128, NCH, NTOK], BF16)
        wq = sb("wq", [128, NCH, C], BF16)
        wka = sb("wka", [128, NCH, LAT], BF16)
        wva = sb("wva", [128, NCH, LAT], BF16)
        wkb = sb("wkb", [128, NLC, C], BF16)
        wvb = sb("wvb", [128, NLC, C], BF16)
        ident = sb("ident", [128, 128], BF16)
        kc = sb("kc", [128, NLC, NTOK], BF16)
        vc = sb("vc", [128, NLC, NTOK], BF16)
        crt0 = sb("crt0", [128, 128], F32)
        srt0 = sb("srt0", [128, 128], F32)
        crt = [crt0, crt0]
        srt = [srt0, srt0]
        qb = [sb(f"qb{j}", [128, C], BF16) for j in range(2)]
        kb = [sb(f"kb{j}", [128, C], BF16) for j in range(2)]
        vb = [sb(f"vb{j}", [128, C], BF16) for j in range(2)]
        ta = sb("ta", [128, 128], F32)
        tb = sb("tb", [128, 128], F32)
        tb2 = sb("tb2", [128, 128], F32)
        slab0 = sb("sl0", [128, NCH, 512], BF16)
        slab20 = sb("sl20", [128, NCH, 512], BF16)
        slab = [slab0, slab0]
        slab2 = [slab20, slab20]
        pq = pb("pq", [128, C])
        pk = pb("pk", [128, C])
        pv = pk
        tp = pb("tp", [128, 1024], BF16)
        pkc = pb("pkc", [128, 512])
        w1g = [sb(f"w1g{j}", [128, NCH, 512], BF16) for j in range(2)]
        w2g = [sb(f"w2g{j}", [128, NCH, 512], BF16) for j in range(2)]
        s1c = sb("s1c", [128, DH // 128], F32)
        s2c = sb("s2c", [128, DH // 128], F32)
        sil = sb("sil", [128, 512], F32)
        hv = [sb(f"hv{j}", [128, 512], BF16) for j in range(2)]
        psA0 = pb("psA0", [128, 512])
        psB0 = pb("psB0", [128, 512])
        psA = [psA0, psA0]
        psB = [psB0, psB0]

        make_identity_bf(nc, sc, ident)
        for n in range(NTOK // 512):
            sc.load(xf[:, :, n * 512:(n + 1) * 512], d["xn1T_bf"][:, :, n * 512:(n + 1) * 512],
                    writes=(f"xf{n}",))
        for ci in range(NCH):
            sc.load(wq[:, ci, :], d["wqT"][ci * 128:(ci + 1) * 128, :], writes=("wq",))
            sc.load(wka[:, ci, :], d["wk_aT"][ci * 128:(ci + 1) * 128, :], writes=("wka",))
            sc.load(wva[:, ci, :], d["wv_aT"][ci * 128:(ci + 1) * 128, :], writes=("wva",))
        for lc in range(NLC):
            sc.load(wkb[:, lc, :], d["wk_bT"][lc * 128:(lc + 1) * 128, :], writes=("wkb",))
            sc.load(wvb[:, lc, :], d["wv_bT"][lc * 128:(lc + 1) * 128, :], writes=("wvb",))
        col_loads(sc, s1c, d["w1s"], "s1c")
        col_loads(sc, s2c, d["w2s"], "s2c")

        qkv_jobs = []
        for n in range(NTOK // 512):
            def latjob(n=n):
                for lc in range(NLC):
                    for wsb, dst, rn in ((wka, kc, "kc"), (wva, vc, "vc")):
                        def mmc(e, lc=lc, wsb=wsb, n=n):
                            for ci in range(NCH):
                                mm = e.matmul(pkc[:], wsb[:, ci, lc * 128:(lc + 1) * 128],
                                              xf[:, ci, n * 512:(n + 1) * 512],
                                              start=(ci == 0), stop=(ci == NCH - 1))
                            return mm
                        sc.add("t", mmc, reads=(f"xf{n}", "wka", "wva"), writes=("pkc",))
                        sc.add("v", lambda e, lc=lc, n=n, dst=dst: e.tensor_copy(
                            dst[:, lc, n * 512:(n + 1) * 512], pkc[:]),
                            reads=("pkc",), writes=(f"{rn}{n}",))
            qkv_jobs.append(latjob)
        for i in range(NTT):
            def tilejob(i=i):
                j = i % 2
                pos = (i % (T // 128)) * 128
                n = i // 4
                sc.load(crt[j][:], d["crep"][pos:pos + 128, :], writes=("crtr",))
                sc.load(srt[j][:], d["srep"][pos:pos + 128, :], writes=("srtr",))

                def mmq(e, i=i):
                    for ci in range(NCH):
                        for h in range(2):
                            mm = e.matmul(pq[:, h * 512:(h + 1) * 512],
                                          xf[:, ci, i * 128:(i + 1) * 128],
                                          wq[:, ci, h * 512:(h + 1) * 512],
                                          start=(ci == 0), stop=(ci == NCH - 1))
                    return mm
                sc.add("t", mmq, reads=(f"xf{i // 4}", "wq"), writes=("pq",))

                def mmv(e, i=i):
                    for lc in range(NLC):
                        for h in range(2):
                            mm = e.matmul(pv[:, h * 512:(h + 1) * 512], vc[:, lc, i * 128:(i + 1) * 128],
                                          wvb[:, lc, h * 512:(h + 1) * 512],
                                          start=(lc == 0), stop=(lc == NLC - 1))
                    return mm
                sc.add("t", mmv, reads=(f"vc{n}", "wvb"), writes=("pk",))
                sc.add("v", lambda e, j=j: e.tensor_copy(vb[j][:], pv[:]), reads=("pk",), writes=(f"vb{j}",))
                sc.store(d["v_bf"][i * 128:(i + 1) * 128, :], vb[j][:], reads=(f"vb{j}",))

                def mmk(e, i=i):
                    for lc in range(NLC):
                        for h in range(2):
                            mm = e.matmul(pk[:, h * 512:(h + 1) * 512], kc[:, lc, i * 128:(i + 1) * 128],
                                          wkb[:, lc, h * 512:(h + 1) * 512],
                                          start=(lc == 0), stop=(lc == NLC - 1))
                    return mm
                sc.add("t", mmk, reads=(f"kc{n}", "wkb"), writes=("pk",))
                for (psrc, obuf, obn) in ((pq, qb, "qb"), (pk, kb, "kb")):
                    pn = "pq" if psrc is pq else "pk"
                    rope3 = lambda t_: t_.rearrange("p (nh hd) -> p nh hd", nh=NH)
                    cr3 = lambda t_: t_.rearrange("p (nh e) -> p nh e", nh=NH)
                    sc.add("v", lambda e, psrc=psrc, obuf=obuf, j=j: e.tensor_copy(obuf[j][:], psrc[:]),
                           reads=(pn,), writes=(f"{obn}{j}",))
                    sc.add("v", lambda e, psrc=psrc, j=j, rope3=rope3, cr3=cr3: e.tensor_mul(
                        cr3(ta[:])[:, :, 0:8], rope3(psrc[:])[:, :, 0:8], cr3(crt[j][:])[:, :, 0:8]),
                        reads=(pn, "crtr"), writes=("ta",))
                    sc.add("v", lambda e, psrc=psrc, j=j, rope3=rope3, cr3=cr3: e.tensor_mul(
                        cr3(tb[:])[:, :, 0:8], rope3(psrc[:])[:, :, 8:16], cr3(srt[j][:])[:, :, 0:8]),
                        reads=(pn, "srtr"), writes=("tb",))
                    sc.add("v", lambda e, cr3=cr3: e.tensor_tensor(
                        cr3(ta[:])[:, :, 0:8], cr3(ta[:])[:, :, 0:8], cr3(tb[:])[:, :, 0:8], ALU.subtract),
                        reads=("ta", "tb"), writes=("ta",))
                    sc.add("v", lambda e, psrc=psrc, j=j, rope3=rope3, cr3=cr3: e.tensor_mul(
                        cr3(tb[:])[:, :, 0:8], rope3(psrc[:])[:, :, 0:8], cr3(srt[j][:])[:, :, 0:8]),
                        reads=(pn, "srtr"), writes=("tb",))
                    sc.add("v", lambda e, psrc=psrc, j=j, rope3=rope3, cr3=cr3: e.tensor_mul(
                        cr3(tb2[:])[:, :, 0:8], rope3(psrc[:])[:, :, 8:16], cr3(crt[j][:])[:, :, 0:8]),
                        reads=(pn, "crtr"), writes=("tb2",))
                    sc.add("v", lambda e, obuf=obuf, j=j, rope3=rope3, cr3=cr3: e.tensor_tensor(
                        rope3(obuf[j][:])[:, :, 8:16], cr3(tb[:])[:, :, 0:8], cr3(tb2[:])[:, :, 0:8], ALU.add),
                        reads=("tb", "tb2"), writes=(f"{obn}{j}",))
                    sc.add("v", lambda e, obuf=obuf, j=j, rope3=rope3, cr3=cr3: e.tensor_copy(
                        rope3(obuf[j][:])[:, :, 0:8], cr3(ta[:])[:, :, 0:8]),
                        reads=("ta",), writes=(f"{obn}{j}",))
                si = (i // 4) % 2
                for (obuf, obn, slb, dn, sn) in ((qb, "qb", slab, "qT_bf", "sl"),
                                                 (kb, "kb", slab2, "kTq_bf", "sl2")):
                    for c in range(NCH):
                        sc.add("t", lambda e, obuf=obuf, j=j, c=c: e.transpose(
                            tp[:, c * 128:(c + 1) * 128], obuf[j][:, c * 128:(c + 1) * 128], ident[:]),
                            reads=(f"{obn}{j}", "ident"), writes=(f"tp{c}",))
                    sc.add("v", lambda e, slb=slb, si=si, i=i: e.tensor_copy(
                        slb[si][:, :, (i % 4) * 128:(i % 4 + 1) * 128],
                        tp[:].rearrange("p (c f) -> p c f", c=NCH)),
                        reads=tuple(f"tp{c}" for c in range(NCH)), writes=(f"{sn}w{si}_{i % 4}",))
                    if i % 4 == 3:
                        i0 = i - 3
                        sc.store(d[dn][:, :, i0 * 128:(i0 + 4) * 128], slb[si][:],
                                 reads=tuple(f"{sn}w{si}_{k}" for k in range(4)))
            qkv_jobs.append(tilejob)

        ab_jobs = []
        w13 = d["w1T"].rearrange("(cb p) o -> p cb o", p=128)
        w23 = d["w2T"].rearrange("(cb p) o -> p cb o", p=128)
        for hg in range(DH // 512):
            def wload(hg=hg):
                jw = hg % 2
                sc.load(w1g[jw][:], w13[:, :, hg * 512:(hg + 1) * 512], writes=(f"w1g{jw}",))
                sc.load(w2g[jw][:], w23[:, :, hg * 512:(hg + 1) * 512], writes=(f"w2g{jw}",))
            ab_jobs.append(wload)
            for hc4 in range(4):
                for n in range(NTOK // 512):
                    def abjob(hg=hg, hc4=hc4, n=n):
                        jw = hg % 2
                        hc = hg * 4 + hc4
                        j = n % 2

                        def mma(e, hc4=hc4, n=n, j=j, jw=jw):
                            for ci in range(NCH):
                                mm = e.matmul(psA[j][:], w1g[jw][:, ci, hc4 * 128:(hc4 + 1) * 128],
                                              xf[:, ci, n * 512:(n + 1) * 512],
                                              start=(ci == 0), stop=(ci == NCH - 1))
                            return mm
                        sc.add("t", mma, reads=(f"w1g{jw}", f"xf{n}"), writes=("psA0",))

                        def mmb(e, hc4=hc4, n=n, j=j, jw=jw):
                            for ci in range(NCH):
                                mm = e.matmul(psB[j][:], w2g[jw][:, ci, hc4 * 128:(hc4 + 1) * 128],
                                              xf[:, ci, n * 512:(n + 1) * 512],
                                              start=(ci == 0), stop=(ci == NCH - 1))
                            return mm
                        sc.add("t", mmb, reads=(f"w2g{jw}", f"xf{n}"), writes=("psB0",))
                        sc.add("s", lambda e, j=j, hc=hc: e.activation(
                            sil[:], psA[j][:], AF.Silu, scale=s1c[:, hc:hc + 1]),
                            reads=("psA0", "s1c"), writes=("sil",))
                        sc.add("v", lambda e, j=j, hc=hc: e.scalar_tensor_tensor(
                            hv[j][:], psB[j][:], s2c[:, hc:hc + 1], sil[:], ALU.mult, ALU.mult),
                            reads=("psB0", "s2c", "sil"), writes=(f"hv{j}",))
                        sc.store(d["hfb_bf"][:, hc, n * 512:(n + 1) * 512], hv[j][:],
                                 reads=(f"hv{j}",), writes=(f"dh_{hc}_{n}",))
                    ab_jobs.append(abjob)
        weave(sc, [qkv_jobs, ab_jobs], [1, 7])
        sc.flush()


def build_L1b(ctx, d):
    nc = ctx.nc
    nm = ctx.name("l1b")
    sc = Sched(ctx, nm)
    NHC = DH // 128
    with ExitStack() as ses:
        sb = lambda n, s, dt: ses.enter_context(nc.sbuf_tensor(nm + n, s, dt))
        pb = lambda n, s, dt=F32: ses.enter_context(nc.psum_tensor(nm + n, s, dt))
        qh = [sb(f"qh{j}", [64, T], BF16) for j in range(2)]
        kh = [sb(f"kh{j}", [64, T], BF16) for j in range(2)]
        vsb = sb("vsb", [128, T // 128, C], BF16)
        oh = [sb(f"oh{j}", [64, T], BF16) for j in range(2)]
        Pt = [sb(f"Pt{j}", [128, T], BF16) for j in range(2)]
        PTt = [sb(f"PTt{j}", [128, T // 2], BF16) for j in range(4)]
        stt_ = [sb(f"stt{j}", [128, 8], F32) for j in range(2)]
        S4 = pb("S4", [128, T])
        psO = pb("psO", [64, T // 2])
        hn = [sb(f"hn{j}", [128, NHC, 512], BF16) for j in range(2)]
        wq_ = sb("wqh", [128, NHC, 1024], BF16)
        ssc = sb("ssc", [128, 2, NHC], F32)
        gv = [sb(f"gv{j}", [128, 512], BF16) for j in range(2)]
        ps = [pb(f"ps{j}", [128, 512]) for j in range(2)]

        col_loads(sc, ssc[:, 0, :], d["h1s"], "ssc1_")
        col_loads(sc, ssc[:, 1, :], d["h2s"], "ssc2_")

        attn_jobs = []

        def emit_loads(b, h):
            jh = h % 2
            hp, hc = (h % 2) * 64, h // 2
            sc.load(qh[jh][:], d["qT_bf"][hp:hp + 64, hc, b * T:(b + 1) * T], writes=(f"qh{jh}",))
            sc.load(kh[jh][:], d["kTq_bf"][hp:hp + 64, hc, b * T:(b + 1) * T], writes=(f"kh{jh}",))

        def emit_vload(b):
            for t in range(0, T // 128, 4):
                sc.load(vsb[:, t:t + 4, :],
                        d["v_bf"][(b * T // 128 + t) * 128:(b * T // 128 + t + 4) * 128, :]
                        .rearrange("(t p) c -> p t c", p=128),
                        writes=("vsb",))

        def emit_score_tile(b, h, tq):
            jh = h % 2
            j = tq % 2
            st_ = stt_[j]
            for half in range(2):
                hr = f"S4{'AB'[half]}"
                for tkc in (0, 1):
                    def mms(e, jh=jh, tq=tq, half=half, tkc=tkc):
                        c = half * 2 + tkc
                        return e.matmul(S4[:, c * 512:(c + 1) * 512],
                                        qh[jh][:, tq * 128:(tq + 1) * 128],
                                        kh[jh][:, c * 512:(c + 1) * 512],
                                        start=True, stop=True)
                    sc.add("t", mms, reads=(f"qh{jh}", f"kh{jh}"), writes=(hr,))
                sc.add("v", lambda e, j=j, half=half, st_=st_: e.reduce_max(
                    st_[:, half:half + 1], S4[:, half * 1024:(half + 1) * 1024],
                    axis=AX.X, negate=True),
                    reads=(hr,), writes=(f"st{j}_{half}",))
                sc.add("v", lambda e, j=j, half=half, st_=st_: e.tensor_scalar(
                    st_[:, half:half + 1], st_[:, half:half + 1], 0.125, None, ALU.mult),
                    reads=(f"st{j}_{half}",), writes=(f"st{j}_{half}",))
                sc.add("s", lambda e, j=j, half=half, st_=st_: e.activation(
                    Pt[j][:, half * 1024:(half + 1) * 1024], S4[:, half * 1024:(half + 1) * 1024],
                    AF.Exp, bias=st_[:, half:half + 1], scale=0.125,
                    accum_out=st_[:, 5 + half:6 + half]),
                    reads=(hr, f"st{j}_{half}"), writes=(f"Pt{j}_{half}", f"st{j}_s{half}"))
            sc.add("v", lambda e, st_=st_: e.tensor_reduce(
                st_[:, 2:3], st_[:, 0:2], AX.X, ALU.min),
                reads=(f"st{j}_0", f"st{j}_1"), writes=(f"st{j}_n",))
            sc.add("v", lambda e, st_=st_: e.tensor_scalar(
                st_[:, 3:5], st_[:, 0:2], st_[:, 2:3], None, ALU.subtract),
                reads=(f"st{j}_n", f"st{j}_0", f"st{j}_1"), writes=(f"st{j}_d",))
            sc.add("s", lambda e, st_=st_: e.activation(
                st_[:, 3:5], st_[:, 3:5], AF.Exp, scale=-1.0),
                reads=(f"st{j}_d",), writes=(f"st{j}_c",))
            sc.add("v", lambda e, st_=st_: e.tensor_mul(st_[:, 5:7], st_[:, 5:7], st_[:, 3:5]),
                   reads=(f"st{j}_c", f"st{j}_s0", f"st{j}_s1"), writes=(f"st{j}_sc",))
            sc.add("v", lambda e, st_=st_: e.reduce_sum(st_[:, 7:8], st_[:, 5:7], axis=AX.X),
                   reads=(f"st{j}_sc",), writes=(f"st{j}_sum",))
            sc.add("v", lambda e, st_=st_: e.reciprocal(st_[:, 7:8], st_[:, 7:8]),
                   reads=(f"st{j}_sum",), writes=(f"st{j}_sum",))
            sc.add("v", lambda e, st_=st_: e.tensor_scalar(st_[:, 3:5], st_[:, 3:5], st_[:, 7:8], None, ALU.mult),
                   reads=(f"st{j}_c", f"st{j}_sum"), writes=(f"st{j}_f",))
            sc.add("v", lambda e, j=j, st_=st_: e.tensor_scalar(
                Pt[j][:, 0:1024], Pt[j][:, 0:1024], st_[:, 3:4], None, ALU.mult),
                reads=(f"Pt{j}_0", f"st{j}_f"), writes=(f"Pt{j}_0",))
            sc.add("v", lambda e, j=j, st_=st_: e.tensor_scalar(
                Pt[j][:, 1024:2048], Pt[j][:, 1024:2048], st_[:, 4:5], None, ALU.mult),
                reads=(f"Pt{j}_1", f"st{j}_f"), writes=(f"Pt{j}_1",))
            hp2 = h % 2
            sc.store(d["P_bf" if hp2 == 0 else "P2_bf"][tq * 128:(tq + 1) * 128, :], Pt[j][:],
                     reads=(f"Pt{j}_0", f"Pt{j}_1"), writes=(f"dP{hp2}_{tq}",), q="g")

        def emit_av(b, h, step):
            hp2 = h % 2
            half, tkc = step // 16, step % 16
            jx = step % 4
            sc.add("y", lambda e, tkc=tkc, jx=jx, hp2=hp2, half=half: e.dma_start(
                out=PTt[jx][:],
                in_=d["P_bf" if hp2 == 0 else "P2_bf"][half * 1024:(half + 1) * 1024,
                                                       tkc * 128:(tkc + 1) * 128],
                transpose=True),
                reads=tuple(f"dP{hp2}_{tq}" for tq in range(T // 128)),
                writes=(f"PTt{jx}",), dma=True)

            def mmav(e, tkc=tkc, jx=jx, h=h):
                for tqc in range(2):
                    mm = e.matmul(psO[:, tqc * 512:(tqc + 1) * 512],
                                  vsb[:, tkc, h * 64:(h + 1) * 64],
                                  PTt[jx][:, tqc * 512:(tqc + 1) * 512],
                                  start=(tkc == 0), stop=(tkc == T // 128 - 1))
                return mm
            sc.add("t", mmav, reads=(f"PTt{jx}", "vsb"), writes=("psO",))
            if tkc == T // 128 - 1:
                jo = h % 2
                hp, hc = (h % 2) * 64, h // 2
                sc.add("v", lambda e, jo=jo, half=half: e.tensor_copy(
                    oh[jo][:, half * 1024:(half + 1) * 1024], psO[:]),
                    reads=("psO",), writes=(f"oh{jo}_{half}",))
                if half == 1:
                    sc.store(d["OT_bf"][hp:hp + 64, hc, b * T:(b + 1) * T], oh[jo][:],
                             reads=(f"oh{jo}_0", f"oh{jo}_1"), q="g")

        def unit(b, h, i):
            def f():
                if i == 0 and h < NH:
                    emit_loads(b, h)
                if h == 0 and i == 0:
                    emit_vload(b)
                if h < NH:
                    emit_score_tile(b, h, i)
                if h > 0:
                    emit_av(b, h - 1, 2 * i)
                    emit_av(b, h - 1, 2 * i + 1)
            return f

        for b in range(B):
            for h in range(NH + 1):
                for i in range(T // 128):
                    attn_jobs.append(unit(b, h, i))

        h_jobs = []
        for ph, win, hin, hout, sidx in (("1", "h1T", "hfb_bf", "g1_bf", 0),
                                         ("2", "h2T", "g1_bf", "g2_bf", 1)):
            win3 = d[win].rearrange("(cb p) o -> p cb o", p=128)
            hin_t = d[hin]
            for coq in range(4):
                for hw in range(2):
                    def wload(coq=coq, hw=hw, win3=win3, ph=ph):
                        for cis in range(0, NHC, 8):
                            sc.load(wq_[:, cis:cis + 8, hw * 512:(hw + 1) * 512],
                                    win3[:, cis:cis + 8, coq * 1024 + hw * 512: coq * 1024 + (hw + 1) * 512],
                                    writes=(f"wqh{hw}_{cis}",))
                    h_jobs.append(wload)
                for n in range(NTOK // 512):
                    def hload(n=n, hin_t=hin_t, ph=ph):
                        jn = n % 2
                        for cis in range(0, NHC, 8):
                            sc.load(hn[jn][:, cis:cis + 8, :],
                                    hin_t[:, cis:cis + 8, n * 512:(n + 1) * 512],
                                    reads=tuple((f"dh_{ci}_{n}" if ph == "1" else f"dg1_{ci}_{n}")
                                                for ci in range(cis, cis + 8)),
                                    writes=(f"hn{jn}_{cis}",))
                    h_jobs.append(hload)
                    for co8 in range(8):
                        def hjob(coq=coq, n=n, co8=co8, ph=ph, hout=hout, sidx=sidx):
                            hw, co4 = co8 // 4, co8 % 4
                            jn = n % 2
                            co = coq * 8 + co8
                            j = co8 % 2

                            def mmh(e, hw=hw, co4=co4, j=j, jn=jn):
                                for ci in range(NHC):
                                    mm = e.matmul(ps[j][:],
                                                  wq_[:, ci, hw * 512 + co4 * 128: hw * 512 + (co4 + 1) * 128],
                                                  hn[jn][:, ci, :], start=(ci == 0), stop=(ci == NHC - 1))
                                return mm
                            sc.add("t", mmh,
                                   reads=tuple(f"wqh{hw}_{cis}" for cis in range(0, NHC, 8)) +
                                         tuple(f"hn{jn}_{cis}" for cis in range(0, NHC, 8)),
                                   writes=(f"ps{j}",))
                            sc.add("s", lambda e, j=j, co=co, sidx=sidx: e.activation(
                                gv[j][:], ps[j][:], AF.Gelu, scale=ssc[:, sidx, co:co + 1]),
                                reads=(f"ps{j}", (f"ssc1_{co}" if sidx == 0 else f"ssc2_{co}")),
                                writes=(f"gv{j}",))
                            sc.store(d[hout][:, co, n * 512:(n + 1) * 512], gv[j][:],
                                     reads=(f"gv{j}",),
                                     writes=((f"dg1_{co}_{n}",) if ph == "1" else (f"dg2_{co}_{n}",)))
                        h_jobs.append(hjob)
        weave(sc, [attn_jobs, h_jobs], [16, 19])
        sc.flush()


def build_L1c(ctx, d):
    nc = ctx.nc
    nm = ctx.name("l1c")
    sc = Sched(ctx, nm)
    NHC = DH // 128
    with ExitStack() as ses:
        sb = lambda n, s, dt: ses.enter_context(nc.sbuf_tensor(nm + n, s, dt))
        pb = lambda n, s, dt=F32: ses.enter_context(nc.psum_tensor(nm + n, s, dt))
        wo = sb("wo", [128, NCH, C], BF16)
        ott = [sb(f"ott{j}", [128, NCH, 512], BF16) for j in range(2)]
        evw = [sb(f"evw{j}", [128, C], F32) for j in range(2)]
        psw = [pb(f"psw{j}", [128, C]) for j in range(2)]
        w3 = sb("w3", [128, NHC, C], BF16)
        w3row = sb("w3row", [1, C], F32)
        onesc = sb("onesc", [1, 128], F32)
        wsb = sb("wsb", [128, C], F32)
        g2t = [sb(f"g2t{j}", [128, NHC, 512], BF16) for j in range(2)]
        o0t = [sb(f"o0t{j}", [128, C], F32) for j in range(2)]
        att = [sb(f"att{j}", [128, C], F32) for j in range(2)]
        ov = [sb(f"ov{j}", [128, C], F32) for j in range(2)]
        ps3 = [pb(f"ps3{j}", [128, C]) for j in range(2)]
        psb = psw[0]
        for c in range(NCH):
            sc.load(wo[:, c, :], d["woT"][c * 128:(c + 1) * 128, :], writes=("wo",))
        for ci in range(NHC):
            sc.load(w3[:, ci, :], d["w3T"][ci * 128:(ci + 1) * 128, :], writes=("w3",))
        sc.load(w3row[:], d["w3s"].unsqueeze(0), writes=("w3row",))
        sc.add("g", lambda e: e.memset(onesc[:], 1.0), writes=("onesc",))

        def mmbc(e):
            for hf in range(2):
                mm = e.matmul(psb[:, hf * 512:(hf + 1) * 512], onesc[:],
                              w3row[:, hf * 512:(hf + 1) * 512], start=True, stop=True)
            return mm
        sc.add("t", mmbc, reads=("onesc", "w3row"), writes=("psw0",))
        sc.add("v", lambda e: e.tensor_copy(wsb[:], psb[:]), reads=("psw0",), writes=("wsb",))

        wo_jobs = []
        for g in range(NTOK // 512):
            def wog(g=g):
                jg = g % 2
                sc.load(ott[jg][:], d["OT_bf"][:, :, g * 512:(g + 1) * 512], writes=(f"ott{jg}",))
                for t4 in range(4):
                    i = g * 4 + t4
                    j = i % 2

                    def mmw(e, j=j, jg=jg, t4=t4):
                        for dc in range(NCH):
                            for h in range(2):
                                mm = e.matmul(psw[j][:, h * 512:(h + 1) * 512],
                                              ott[jg][:, dc, t4 * 128:(t4 + 1) * 128],
                                              wo[:, dc, h * 512:(h + 1) * 512],
                                              start=(dc == 0), stop=(dc == NCH - 1))
                        return mm
                    sc.add("t", mmw, reads=(f"ott{jg}", "wo"), writes=(f"psw{j}",))
                    sc.add("v", lambda e, j=j: e.tensor_copy(evw[j][:], psw[j][:]),
                           reads=(f"psw{j}",), writes=(f"evw{j}",))
                    sc.store(d["att_f32"][i * 128:(i + 1) * 128, :], evw[j][:],
                             reads=(f"evw{j}",), writes=(f"datt{i}",))
            wo_jobs.append(wog)

        w3_jobs = []
        for g in range(NTOK // 512):
            def w3g(g=g):
                jg = g % 2
                for cis in range(0, NHC, 8):
                    sc.load(g2t[jg][:, cis:cis + 8, :], d["g2_bf"][:, cis:cis + 8, g * 512:(g + 1) * 512],
                            reads=tuple(f"dg2_{ci}_{g}" for ci in range(cis, cis + 8)),
                            writes=(f"g2t{jg}",))
                for t4 in range(4):
                    i = g * 4 + t4
                    j = i % 2
                    sc.load(o0t[j][:], d["out0_f32"][i * 128:(i + 1) * 128, :], writes=(f"o0t{j}",))
                    sc.load(att[j][:], d["att_f32"][i * 128:(i + 1) * 128, :],
                            reads=(f"datt{i}",), writes=(f"att{j}",))

                    def mmw3(e, t4=t4, j=j, jg=jg):
                        for ci in range(NHC):
                            for hf in range(2):
                                mm = e.matmul(ps3[j][:, hf * 512:(hf + 1) * 512],
                                              g2t[jg][:, ci, t4 * 128:(t4 + 1) * 128],
                                              w3[:, ci, hf * 512:(hf + 1) * 512],
                                              start=(ci == 0), stop=(ci == NHC - 1))
                        return mm
                    sc.add("t", mmw3, reads=(f"g2t{jg}", "w3"), writes=(f"ps3{j}",))
                    sc.add("v", lambda e, j=j: e.tensor_tensor(ov[j][:], ps3[j][:], wsb[:], ALU.mult),
                           reads=(f"ps3{j}", "wsb"), writes=(f"ov{j}",))
                    sc.add("v", lambda e, j=j: e.tensor_tensor(ov[j][:], ov[j][:], o0t[j][:], ALU.add),
                           reads=(f"ov{j}", f"o0t{j}"), writes=(f"ov{j}",))
                    sc.add("v", lambda e, j=j: e.tensor_tensor(ov[j][:], ov[j][:], att[j][:], ALU.add),
                           reads=(f"ov{j}", f"att{j}"), writes=(f"ov{j}",))
                    sc.store(d["out"][i * 128:(i + 1) * 128, :], ov[j][:], reads=(f"ov{j}",))
            w3_jobs.append(w3g)
        weave(sc, [wo_jobs, w3_jobs], [1, 1])
        sc.flush()


def build_all(ctx, d):
    build_ln_section(ctx, d, "x", "xn_f32", "xnT_bf", "hT_bf", "ln0")
    build_kvr(ctx, d)
    build_wkv(ctx, d)
    build_moe(ctx, d)
    build_cmix(ctx, d)
    build_close0_ln1(ctx, d)
    build_L1a(ctx, d)
    build_L1b(ctx, d)
    build_L1c(ctx, d)

from concourse.bass_utils import run_bass_kernel_spmd


def kernel(**inputs):
    w, shards = host_prep(inputs)
    nc = bass.Bass("TRN2", target_bir_lowering=False)
    d = declare_drams(nc, inspect=())
    ctx = Ctx(nc)
    build_all(ctx, d)
    in_maps = []
    for c in range(8):
        m = dict(w)
        m["x"] = shards[c]
        in_maps.append(m)
    res = run_bass_kernel_spmd(nc, in_maps, core_ids=list(range(8)))
    out = np.concatenate([res.results[c]["out"].reshape(2, T, C) for c in range(8)], axis=0)
    return out.astype(np.float32)
